# version 24
# baseline (speedup 1.0000x reference)
import math
import numpy as np
import ml_dtypes
import concourse.bass as bass
import concourse.mybir as mybir
from concourse.bass_utils import run_bass_kernel_spmd

F32 = mybir.dt.float32
BF16 = mybir.dt.bfloat16
AF = mybir.ActivationFunctionType
ALU = mybir.AluOpType
AX = mybir.AxisListType

BF = ml_dtypes.bfloat16

NCORES = 8
NTOK = 4096
DM = 1024
CH = 8
NT = 8
TT = 512
NL = 2
HEADS = 8
DH = 64
INNER = 512
HID = 2730
NIH = 22
HIDP = NIH * 128
WIN = 512
QT = NTOK // 128
SLAB = NTOK + 1
MASKNEG = -1e30
CTXSCALE = math.sqrt(3.0 / 64.0)

ROUNDS = [list(range(0, 4)), list(range(4, 8)), list(range(8, 12)),
          list(range(12, 16)), list(range(16, 19)), list(range(19, 22))]
NQ = 4
QTT = 2


class Sched:
    TAPES = ("T", "V", "S", "G", "Y")

    def __init__(self):
        self.ops = []
        self.tape_ops = {t: [] for t in self.TAPES}
        self.res = {}
        self.floor = []

    def op(self, tape, fn, reads=(), writes=(), dma=False, dkey=None,
           after=(), tag="", multi=False):
        if dma:
            assert dkey is not None, tag
            key = "D." + dkey
        else:
            key = tape
        oid = len(self.ops)
        deps = set()
        for a in after:
            if a is not None:
                deps.add(a)
        for r in reads:
            st = self.res.get(r)
            if st is not None and st["w"] is not None:
                deps.add(st["w"])
        for w in writes:
            st = self.res.get(w)
            if st is not None:
                if st["w"] is not None:
                    deps.add(st["w"])
                for r in st["r"]:
                    deps.add(r)
        deps.update(self.floor)
        deps.discard(oid)
        o = dict(id=oid, tape=tape, key=key, fn=fn, deps=deps, dma=dma,
                 flag=False, cnt=None, tag=tag, multi=multi)
        self.ops.append(o)
        self.tape_ops[tape].append(o)
        for r in reads:
            self.res.setdefault(r, {"w": None, "r": []})["r"].append(oid)
        for w in writes:
            st = self.res.setdefault(w, {"w": None, "r": []})
            st["w"] = oid
            st["r"] = []
        return oid

    def fence(self):
        self.floor = [t[-1]["id"] for t in self.tape_ops.values() if t]

    def fence_from(self, mark, extra=()):
        lasts = []
        for t in self.tape_ops.values():
            for o in reversed(t):
                if o["id"] >= mark:
                    if o["tag"].startswith("prep"):
                        continue
                    lasts.append(o["id"])
                    break
        lasts.extend(x for x in extra if x is not None)
        self.floor = lasts

    def last(self, tape):
        t = self.tape_ops[tape]
        return t[-1]["id"] if t else None

    def finalize(self):
        for o in self.ops:
            if o["dma"]:
                o["flag"] = True
            for d in o["deps"]:
                self.ops[d]["flag"] = True
        counts = {}
        for tape in self.TAPES:
            for o in self.tape_ops[tape]:
                if o["flag"]:
                    k = o["key"]
                    amt = 16 if o["dma"] else 1
                    counts[k] = counts.get(k, 0) + amt
                    o["cnt"] = counts[k]
        self.key_totals = counts
        prefix = {}
        atcnt = {}
        dma_running = {}
        for o in self.ops:
            need = {}
            for d in o["deps"]:
                do = self.ops[d]
                assert do["flag"] and do["cnt"] is not None, (do["tag"], do["tape"])
                k, c = do["key"], do["cnt"]
                if do["dma"]:
                    c = dma_running[k]
                if need.get(k, 0) < c:
                    need[k] = c
            cands = sorted(need.items())
            if len(cands) > 1:
                kept = list(cands)
                i = 0
                while i < len(kept):
                    k, c = kept[i]
                    cover = 0
                    for j, (k2, c2) in enumerate(kept):
                        if j == i:
                            continue
                        cover = max(cover, atcnt[(k2, c2)].get(k, 0))
                    if cover >= c:
                        kept.pop(i)
                    else:
                        i += 1
                cands = kept
            o["waits"] = cands
            know = {}
            for k, c in cands:
                for k2, c2 in atcnt[(k, c)].items():
                    if know.get(k2, 0) < c2:
                        know[k2] = c2
            if o["flag"]:
                k = o["key"]
                know[k] = max(know.get(k, 0), o["cnt"])
                pj = prefix.setdefault(k, {})
                for k2, c2 in know.items():
                    if pj.get(k2, 0) < c2:
                        pj[k2] = c2
                atcnt[(k, o["cnt"])] = dict(pj)
                if o["dma"]:
                    dma_running[k] = o["cnt"]

    def run_tape(self, tape, eng, sems):
        for o in self.tape_ops[tape]:
            if o["multi"] or len(o["waits"]) > 1:
                for k, v in o["waits"][:-1]:
                    eng.wait_ge(sems[k], v)
                if o["multi"]:
                    if o["waits"]:
                        k, v = o["waits"][-1]
                        eng.wait_ge(sems[k], v)
                    inst = o["fn"](eng)
                else:
                    inst = o["fn"](eng)
                    k, v = o["waits"][-1]
                    inst._wait_ge(sems[k], v)
            else:
                inst = o["fn"](eng)
                if o["waits"]:
                    k, v = o["waits"][0]
                    inst._wait_ge(sems[k], v)
            if o["flag"]:
                amt = 16 if o["dma"] else 1
                inst.then_inc(sems[o["key"]], amt)


def _pack_w1(w1):
    a = np.zeros((DM, 2 * HIDP), dtype=np.float32)
    a[:, :HID] = w1[:, :HID]
    a[:, HIDP:HIDP + HID] = w1[:, HID:]
    b = a.reshape(CH, 128, 2, NIH, 128)
    out = np.ascontiguousarray(b.transpose(3, 1, 0, 2, 4))
    return out.reshape(NIH, 128, 2048)


def _pack_w2(w2):
    a = np.zeros((HIDP, DM), dtype=np.float32)
    a[:HID] = w2
    b = a.reshape(NIH, 128, CH, 128)
    return np.ascontiguousarray(b).reshape(NIH, 128, DM)


def _pack_kout(w, dout):
    return np.ascontiguousarray(w.reshape(CH, 128, dout))


def _host_tables():
    half = np.arange(0, DH, 2, dtype=np.float64)
    inv_freq = 1.0 / (10000.0 ** (half / DH))
    t = np.arange(NTOK, dtype=np.float64)
    freqs = t[:, None] * inv_freq[None, :]
    freqs = np.concatenate([freqs, freqs], axis=1)
    base = (half + 0.4 * DH) / (1.4 * DH)
    power = (t - NTOK // 2) / float(WIN // 2)
    scale = base[None, :] ** power[:, None]
    scale = np.concatenate([scale, scale], axis=1)
    cos, sin = np.cos(freqs), np.sin(freqs)
    sign = np.ones((1, DH)); sign[0, :DH // 2] = -1.0
    t1q = (cos * scale).T
    t2q = (sign.T * (sin * scale).T)
    t1k = (cos / scale).T
    t2k = (sign.T * (sin / scale).T)
    def tile(x):
        return np.ascontiguousarray(
            np.concatenate([x, x], axis=0).astype(BF))
    return tile(t1q), tile(t2q), tile(t1k), tile(t2k)


def _host_masks():
    m = np.zeros((128, 256), dtype=BF)
    r = np.arange(128)
    lo = np.where(r[None, :] < r[:, None], MASKNEG, 0.0)
    hi = np.where(r[None, :] > r[:, None], MASKNEG, 0.0)
    m[:, :128] = lo.astype(BF)
    m[:, 128:] = hi.astype(BF)
    return m


def _prep_host(inputs):
    d = {}
    d["uWqkv"] = np.stack([_pack_kout(inputs["L0_ffn_wqkv"][l], 1536)
                           for l in range(NL)])
    d["uWo"] = np.stack([np.ascontiguousarray(
        inputs["L0_ffn_wo"][l].reshape(4, 128, DM)) for l in range(NL)])
    d["uW1"] = np.stack([_pack_w1(inputs["L0_ffn_w1"][l]) for l in range(NL)])
    d["uW2"] = np.stack([_pack_w2(inputs["L0_ffn_w2"][l]) for l in range(NL)])
    aw = np.concatenate([inputs["L0_attn_wq"].T, inputs["L0_attn_wk"].T,
                         inputs["L0_attn_wv"].T], axis=1)
    d["uAw"] = np.ascontiguousarray(aw.reshape(CH, 128, 192))
    awo = inputs["L0_attn_wo"].T
    d["uAwo"] = np.ascontiguousarray(np.concatenate([awo, awo], axis=0))
    lng = inputs["L0_ffn_ln_g"]; lnb = inputs["L0_ffn_ln_b"]
    d["uLng"] = np.ascontiguousarray(
        np.stack([lng[l].reshape(CH, 128).T for l in range(NL)], 1).reshape(128, NL * CH))
    d["uLnb"] = np.ascontiguousarray(
        np.stack([lnb[l].reshape(CH, 128).T for l in range(NL)], 1).reshape(128, NL * CH))
    b1 = inputs["L0_ffn_b1"]
    b1p = np.zeros((NL, 2, HIDP), np.float32)
    b1p[:, 0, :HID] = b1[:, :HID]; b1p[:, 1, :HID] = b1[:, HID:]
    d["uB1"] = np.ascontiguousarray(
        b1p.reshape(NL, 2, NIH, 128).transpose(3, 0, 2, 1).reshape(128, NL * NIH * 2))
    b2 = inputs["L0_ffn_b2"]
    d["uB2"] = np.ascontiguousarray(
        b2.reshape(NL, CH, 128).transpose(2, 0, 1).reshape(128, NL * CH))
    qs = inputs["L0_ffn_q_scale"]; ks = inputs["L0_ffn_k_scale"]
    def qcols(x):
        c = np.tile(x, (1, 2))
        return np.ascontiguousarray(c.T)
    def qcols_rot(x):
        r = np.concatenate([x[:, DH // 2:], x[:, :DH // 2]], axis=1)
        return qcols(r)
    d["uQs"] = qcols(qs); d["uKs"] = qcols(ks)
    d["uQsr"] = qcols_rot(qs); d["uKsr"] = qcols_rot(ks)
    t1q, t2q, t1k, t2k = _host_tables()
    d["uTab"] = np.ascontiguousarray(np.stack([t1q, t2q, t1k, t2k]))
    d["uMask"] = _host_masks()
    d["uIdent"] = np.ascontiguousarray(np.eye(128, dtype=BF))
    g2t = np.zeros((2, 128), dtype=BF)
    g2t[0, :64] = 1; g2t[1, 64:] = 1
    d["uG2T"] = g2t
    g2 = np.zeros((128, 2), dtype=BF)
    g2[:64, 0] = 1; g2[64:, 1] = 1
    d["uG2"] = g2
    perm = np.zeros((128, 128), dtype=BF)
    for m in range(128):
        g = m // 64; dd = m % 64
        sd = dd + 32 if dd < 32 else dd - 32
        perm[g * 64 + sd, m] = 1
    d["uPerm"] = perm
    return d


def build_program(dump=None):
    nc = bass.Bass("TRN2", target_bir_lowering=False)

    def din(name, shape, dt=F32):
        return nc.dram_tensor(name, list(shape), dt, kind="ExternalInput").ap()

    x_t = din("x_t", (DM, NTOK))
    uWqkv = din("uWqkv", (NL, CH, 128, 1536))
    uWo = din("uWo", (NL, 4, 128, DM))
    uW1 = din("uW1", (NL, NIH, 128, 2048))
    uW2 = din("uW2", (NL, NIH, 128, DM))
    uAw = din("uAw", (CH, 128, 192))
    uAwo = din("uAwo", (128, DM))
    uLng = din("uLng", (128, NL * CH))
    uLnb = din("uLnb", (128, NL * CH))
    uB1 = din("uB1", (128, NL * NIH * 2))
    uB2 = din("uB2", (128, NL * CH))
    uQs = din("uQs", (128, NL)); uKs = din("uKs", (128, NL))
    uQsr = din("uQsr", (128, NL)); uKsr = din("uKsr", (128, NL))
    uTab = din("uTab", (4, 128, NTOK), BF16)
    uMask = din("uMask", (128, 256), BF16)
    uIdent = din("uIdent", (128, 128), BF16)
    uG2T = din("uG2T", (2, 128), BF16)
    uG2 = din("uG2", (128, 2), BF16)
    uPerm = din("uPerm", (128, 128), BF16)

    o_dm = nc.dram_tensor("o_dm", [DM, NTOK], F32, kind="ExternalOutput").ap()
    dbg = None
    if dump is not None:
        dbg = nc.dram_tensor("dbg", [128, dump[1]], F32, kind="ExternalOutput").ap()

    zWqkv = nc.dram_tensor("zWqkv", [NL, CH, 128, 1536], BF16, kind="Internal").ap()
    zWo = nc.dram_tensor("zWo", [NL, 4, 128, DM], BF16, kind="Internal").ap()
    zW1 = nc.dram_tensor("zW1", [NL, NIH, 128, 2048], BF16, kind="Internal").ap()
    zW2 = nc.dram_tensor("zW2", [NL, NIH, 128, DM], BF16, kind="Internal").ap()
    zRows = nc.dram_tensor("zRows", [NL, 8, 1024], F32, kind="Internal").ap()

    SA = nc.alloc_sbuf_tensor
    S = SA("S", [128, CH * SLAB], F32)
    ctx_sb = SA("ctx_sb", [128, 2048], BF16)
    woT_sb = SA("woT_sb", [128, DM], BF16)
    c2_sb = SA("c2_sb", [128, 2 * NIH], F32)
    mask_sb = SA("mask_sb", [128, 256], BF16)
    ident_sb = SA("ident_sb", [128, 128], BF16)
    perm_sb = SA("perm_sb", [128, 128], BF16)
    g2_sb = SA("g2_sb", [128, 2], BF16)
    g2t_sb = SA("g2t_sb", [2, 128], BF16)
    onesrow = SA("onesrow", [1, 128], F32)
    onescol = SA("onescol", [128, 1], BF16)
    onesrow_b = SA("onesrow_b", [1, 128], BF16)
    qs_sb = SA("qs_sb", [128, NL], F32); ks_sb = SA("ks_sb", [128, NL], F32)
    qsr_sb = SA("qsr_sb", [128, NL], F32); ksr_sb = SA("ksr_sb", [128, NL], F32)
    lng_sb = SA("lng_sb", [128, NL * CH], F32)
    lnb_sb = SA("lnb_sb", [128, NL * CH], F32)
    lnbb_sb = SA("lnbb_sb", [128, NL * CH], BF16)
    b1_sb = SA("b1_sb", [128, NL * NIH * 2], F32)
    b2_sb = SA("b2_sb", [128, NL * CH], F32)
    pstg = SA("pstg", [128, 2 * 512], F32)
    pout = SA("pout", [128, 2 * 512], BF16)
    rowtmp = SA("rowtmp", [1, 1024], F32)

    probe = SA("probe", [128, 8], F32)
    base = nc.lookup_mloc(probe).addr + 32
    REGION = 65536

    def mkcursor():
        cur = [0]
        def rg(name, shape, dt):
            per = int(np.prod(shape[1:])) * mybir.dt.size(dt)
            off = (cur[0] + 31) & ~31
            cur[0] = off + per
            assert cur[0] <= REGION, (name, off, per, cur[0])
            return nc.alloc_sbuf_tensor_at(name, list(shape), dt,
                                           offset=base + off)
        def at(name, shape, dt, off):
            per = int(np.prod(shape[1:])) * mybir.dt.size(dt)
            assert off + per <= REGION, (name, off, per)
            return nc.alloc_sbuf_tensor_at(name, list(shape), dt,
                                           offset=base + off)
        return rg, at, cur


    r0, at0, c0 = mkcursor()
    xstage = r0("xstage", [128, 2 * 2048], F32)
    qdm = r0("qdm", [128, 2048], BF16)
    kdm = r0("kdm", [128, 2048], BF16)
    vdm = r0("vdm", [128, 2048], BF16)
    ktm2 = r0("ktm2", [128, 2048], BF16)
    vtm2 = r0("vtm2", [128, 2048], BF16)
    E_sb = r0("E_sb", [128, 64], BF16)
    awT_sb = r0("awT_sb", [128, 1536], BF16)
    mu_s0 = r0("mu_s0", [128, 8 * TT], BF16)
    mus0b = r0("mus0b", [128, 8 * TT], BF16)
    b05scr = at0("b05scr", [128, 1024], BF16, 0)
    qdm_off = 16384
    p0cast = at0("p0cast", [128, 1024], BF16, qdm_off)
    p0sq = at0("p0sq", [128, 1024], BF16, qdm_off + 2048)
    foldt = at0("foldt", [128, 512], BF16, qdm_off + 4096)
    foldt2 = at0("foldt2", [128, 512], BF16, qdm_off + 5120)
    assert c0[0] <= REGION, c0

    ra, _, ca = mkcursor()
    wqkvr = ra("wqkvr", [128, 2 * 768], BF16)
    wor = ra("wor", [128, 2 * 512], BF16)
    tabst = ra("tabst", [128, 4 * TT], BF16)
    qrot = ra("qrot", [128, 4 * TT], BF16)
    krot = ra("krot", [128, 4 * 1024], BF16)
    vtmb = ra("vtmb", [128, 4096], BF16)
    osb = ra("osb", [128, 4 * TT], BF16)
    scr = ra("scr", [128, 4 * 512], BF16)
    rrow = ra("rrow", [1, 1024], BF16)
    vtmp = ra("vtmp", [128, 4 * TT], BF16)
    m1s = ra("m1s", [128, 3 * TT], BF16)
    p_sb = ra("p_sb", [128, 2 * 640], BF16)
    pt_sb = ra("pt_sb", [128, 2 * 640], BF16)
    sqs = ra("sqs", [128, 3 * TT], BF16)
    rowkb = ra("rowkb", [2, 512], BF16)
    rowkb2 = ra("rowkb2", [2, 512], BF16)
    rowkb3 = ra("rowkb3", [2, 512], BF16)
    negm_sb = ra("negm_sb", [128, 2], F32)
    rs_sb = ra("rs_sb", [128, 8], F32)
    rr_sb = ra("rr_sb", [128, 8], F32)
    rrb_sb = ra("rrb_sb", [128, 8], BF16)
    assert ca[0] <= REGION, ca

    rf, atf, cf = mkcursor()
    movc = rf("movc", [128, CH * 1024], BF16)
    NPC = max(len(r) for r in ROUNDS)
    gated = rf("gated", [128, NPC * 2 * TT], BF16)
    w2res = rf("w2res", [128, NPC * DM], BF16)
    w1r = rf("w1r", [128, 3 * 2048], BF16)
    gels = rf("gels", [128, 2 * TT], BF16)
    delta = rf("delta", [128, 3 * TT], BF16)
    hcast = atf("hcast", [128, 2 * TT], BF16, 0)
    sqcast = atf("sqcast", [128, 2 * TT], BF16, 2048)
    movtmp = atf("movtmp", [128, TT], F32, 58368)
    assert cf[0] <= REGION, cf

    rz, _, cz = mkcursor()
    cz[0] = (cf[0] + 31) & ~31
    xst2 = rz("xst2", [128, 2 * TT], F32)
    outsb = rz("outsb", [128, 2 * TT], F32)
    assert cz[0] <= REGION, cz

    assert nc.sbuf_bytes_remaining >= 0

    PSIM = nc.alloc_psum_tensor("PSIM", [128, 1536], F32)
    PQ = nc.alloc_psum_tensor("PQ", [128, 1024], F32)
    PMISC = nc.alloc_psum_tensor("PMISC", [128, 512], F32)
    PTA = nc.alloc_psum_tensor("PTA", [128, 640], BF16)
    PTB = nc.alloc_psum_tensor("PTB", [128, 640], BF16)

    sch = Sched()
    OP = sch.op
    sch = Sched()
    OP = sch.op
    T, V, SS, G, Y = "T", "V", "S", "G", "Y"

    def d_name(t):
        return t.ap().tensor.name if hasattr(t, "ap") else t.tensor.name

    PS_SLOT = {0: PSIM[:, 0:512], 1: PSIM[:, 512:1024], 2: PSIM[:, 1024:1536],
               3: PQ[:, 0:512], 4: PQ[:, 512:1024], 5: PMISC[:, :]}
    RS_SLOT = {0: "pPSIM.0", 1: "pPSIM.1", 2: "pPSIM.2",
               3: "pPQ.0", 4: "pPQ.1", 5: "pPMISC"}

    for dst, src in ((mask_sb, uMask), (ident_sb, uIdent), (perm_sb, uPerm),
                     (g2_sb, uG2), (g2t_sb, uG2T), (qs_sb, uQs), (ks_sb, uKs),
                     (qsr_sb, uQsr), (ksr_sb, uKsr), (lng_sb, uLng),
                     (lnb_sb, uLnb), (b1_sb, uB1), (b2_sb, uB2)):
        OP(G, lambda e, d=dst, s=src: e.dma_start(out=d[:], in_=s),
           dma=True, dkey="smalls", writes=(dst.name,))
    OP(V, lambda e: e.memset(onesrow[:], 1.0), writes=("onesrow",))
    OP(V, lambda e: e.memset(onescol[:], 1.0), writes=("onescol",))
    OP(V, lambda e: e.memset(onesrow_b[:], 1.0), writes=("onesrow_b",))
    zap = bass.AP(S.ap().tensor, 0, [[SLAB * CH, 128], [SLAB, CH], [1, 1]])
    MEMSET_S = OP(V, lambda e: e.memset(zap, 0.0), writes=("Szero",))
    OP(G, lambda e: e.tensor_copy(out=lnbb_sb[:], in_=lnb_sb[:]),
       reads=("lnb_sb",), writes=("lnbb_sb",))

    prep_n = [0]
    prep_tasks = []

    def _prep_piece_now(src_ap, dst_ap, width, scale_ap, wres, store_key,
                        ce="G"):
        i = prep_n[0]; prep_n[0] += 1
        sl = (i % 2) * 512
        OP(G, lambda e, s=src_ap, o=sl, w=width:
           e.dma_start(out=pstg[:, o:o + w], in_=s),
           dma=True, dkey="pstg%d" % (i % 2),
           writes=("pstg%d" % (i % 2),), tag="prepL")
        ctape = {"S": SS, "V": V, "G": G}[ce]
        if dst_ap is None:
            home, rname = wres
            if ce == "S":
                OP(SS, lambda e, s=sl, w=width, h=home:
                   e.activation(h, pstg[:, s:s + w], AF.Copy),
                   reads=("pstg%d" % (i % 2),), writes=(rname,), tag="prepC")
            else:
                OP(ctape, lambda e, s=sl, w=width, h=home:
                   e.tensor_copy(h, pstg[:, s:s + w]),
                   reads=("pstg%d" % (i % 2),), writes=(rname,), tag="prepC")
            return
        if scale_ap is None:
            if ce == "S":
                OP(SS, lambda e, s=sl, w=width:
                   e.activation(pout[:, s:s + w], pstg[:, s:s + w], AF.Copy),
                   reads=("pstg%d" % (i % 2),), writes=("pout%d" % (i % 2),),
                   tag="prepC")
            else:
                OP(ctape, lambda e, s=sl, w=width:
                   e.tensor_copy(pout[:, s:s + w], pstg[:, s:s + w]),
                   reads=("pstg%d" % (i % 2),), writes=("pout%d" % (i % 2),),
                   tag="prepC")
        else:
            if ce == "S":
                def cfn(e, s=sl, w=width, scs=scale_ap):
                    n = len(scs); step = width // n
                    for t, sc in enumerate(scs):
                        r = e.activation(pout[:, s + t * step:s + (t + 1) * step],
                                         pstg[:, s + t * step:s + (t + 1) * step],
                                         AF.Copy, scale=sc)
                    return r
                OP(SS, cfn, reads=("pstg%d" % (i % 2),),
                   writes=("pout%d" % (i % 2),), tag="prepC", multi=True)
            else:
                def cfn(e, s=sl, w=width, scs=scale_ap):
                    n = len(scs); step = width // n
                    for t, sc in enumerate(scs):
                        r = e.tensor_scalar_mul(pout[:, s + t * step:s + (t + 1) * step],
                                                pstg[:, s + t * step:s + (t + 1) * step],
                                                sc)
                    return r
                OP(ctape, cfn, reads=("pstg%d" % (i % 2),),
                   writes=("pout%d" % (i % 2),), tag="prepC", multi=True)
        OP(G, lambda e, s=sl, w=width, d=dst_ap:
           e.dma_start(out=d, in_=pout[:, s:s + w]),
           dma=True, dkey=store_key, reads=("pout%d" % (i % 2),),
           writes=wres, tag="prepS")

    def prep_piece(*a, **kw):
        prep_tasks.append((a, kw))

    def pump_prep(n=10**9, ce="G"):
        k = 0
        while prep_tasks and k < n:
            a, kw = prep_tasks.pop(0)
            _prep_piece_now(*a, ce=ce, **(kw or {}))
            k += 1

    def pump_until(total, ce="G"):
        while prep_tasks and prep_n[0] < total:
            a, kw = prep_tasks.pop(0)
            _prep_piece_now(*a, ce=ce, **(kw or {}))

    def emit_prep_layer(l):
        for k in range(CH):
            for pi in range(3):
                prep_piece(uWqkv[l, k, :, pi * 512:(pi + 1) * 512],
                           zWqkv[l, k, :, pi * 512:(pi + 1) * 512], 512, None,
                           ("zWqkv.%d.%d.%d" % (l, k, pi),), "zqkv%d" % l)
        for k in range(4):
            for pi in range(2):
                prep_piece(uWo[l, k, :, pi * 512:(pi + 1) * 512],
                           zWo[l, k, :, pi * 512:(pi + 1) * 512], 512, None,
                           ("zWo.%d.%d.%d" % (l, k, pi),), "zwo%d" % l)
        for ih in range(NIH):
            for pi in range(4):
                scs = [lng_sb[:, l * CH + 2 * pi:l * CH + 2 * pi + 1],
                       lng_sb[:, l * CH + 2 * pi + 1:l * CH + 2 * pi + 2]]
                prep_piece(uW1[l, ih, :, pi * 512:(pi + 1) * 512],
                           zW1[l, ih, :, pi * 512:(pi + 1) * 512], 512, scs,
                           ("zW1.%d.%d.%d" % (l, ih, pi),), "zw1%d" % l)
        for ih in range(NIH):
            for pi in range(2):
                prep_piece(uW2[l, ih, :, pi * 512:(pi + 1) * 512],
                           zW2[l, ih, :, pi * 512:(pi + 1) * 512], 512, None,
                           ("zW2.%d.%d.%d" % (l, ih, pi),), "zw2%d" % l)

    def emit_prep_attnw():
        for k in range(CH):
            prep_piece(uAw[k], None, 192, None,
                       (awT_sb[:, k * 192:(k + 1) * 192], "awT.%d" % k), None)
        for pi in range(2):
            prep_piece(uAwo[:, pi * 512:(pi + 1) * 512], None, 512, None,
                       (woT_sb[:, pi * 512:(pi + 1) * 512], "woT.%d" % pi), None)

    def emit_phase0():
        INV = 1.0 / float(DM)
        for hh in range(2):
            for pr in range(2):
                tts = (hh * 4 + 2 * pr, hh * 4 + 2 * pr + 1)
                for c in range(CH):
                    sl = c % 2
                    OP(SS if sl == 0 else Y, lambda e, c=c, hh=hh, sl=sl:
                       e.dma_start(out=xstage[:, sl * 2048:(sl + 1) * 2048],
                                   in_=x_t[c * 128:(c + 1) * 128,
                                           hh * 2048:(hh + 1) * 2048]),
                       dma=True, dkey="xst%d" % sl,
                       writes=("xst%d" % sl,), tag="xload")
                    for q, tt in enumerate(tts):
                        xoff = sl * 2048 + (tt % 4) * 512
                        OP(SS, lambda e, xo=xoff, q=q, sl=sl:
                           e.activation(p0cast[:, q * 512:(q + 1) * 512],
                                        xstage[:, xo:xo + 512], AF.Copy),
                           reads=("xst%d" % sl,), writes=("p0c%d" % q,))
                        OP(SS, lambda e, xo=xoff, q=q, sl=sl:
                           e.activation(p0sq[:, q * 512:(q + 1) * 512],
                                        xstage[:, xo:xo + 512], AF.Square),
                           reads=("xst%d" % sl,), writes=("p0s%d" % q,))
                        OP(T, lambda e, q=q, c=c:
                           e.matmul(PS_SLOT[2 * q], ident_sb[:],
                                    p0cast[:, q * 512:(q + 1) * 512],
                                    start=(c == 0), stop=(c == 7)),
                           reads=("p0c%d" % q, "ident_sb"),
                           writes=(RS_SLOT[2 * q],))
                        OP(T, lambda e, q=q, c=c:
                           e.matmul(PS_SLOT[2 * q + 1], ident_sb[:],
                                    p0sq[:, q * 512:(q + 1) * 512],
                                    start=(c == 0), stop=(c == 7)),
                           reads=("p0s%d" % q, "ident_sb"),
                           writes=(RS_SLOT[2 * q + 1],))
                for q, tt in enumerate(tts):
                    OP(V, lambda e, q=q: e.tensor_copy(foldt[:], PS_SLOT[2 * q]),
                       reads=(RS_SLOT[2 * q],), writes=("foldt",))
                    OP(V, lambda e, q=q: e.tensor_copy(foldt2[:], PS_SLOT[2 * q + 1]),
                       reads=(RS_SLOT[2 * q + 1],), writes=("foldt2",))
                    OP(T, lambda e: e.matmul(PMISC[0:1, :], onescol[:], foldt[:],
                                             start=True, stop=True),
                       reads=("foldt", "onescol"), writes=("pPMISC",))
                    OP(T, lambda e: e.matmul(PSIM[0:1, 0:512], onescol[:], foldt2[:],
                                             start=True, stop=True),
                       reads=("foldt2", "onescol"), writes=("pPSIM.0",))
                    OP(V, lambda e: e.tensor_scalar_mul(rowtmp[0:1, 0:512],
                                                        PMISC[0:1, :], INV),
                       reads=("pPMISC",), writes=("rowmu",))
                    OP(V, lambda e: e.tensor_mul(rowtmp[0:1, 512:1024],
                                                 rowtmp[0:1, 0:512],
                                                 rowtmp[0:1, 0:512]),
                       reads=("rowmu",), writes=("rowm2",))
                    OP(V, lambda e: e.tensor_scalar(PSIM[0:1, 0:512],
                                                    PSIM[0:1, 0:512], INV, 1e-5,
                                                    ALU.mult, ALU.add),
                       reads=("pPSIM.0",), writes=("pPSIM.0",))
                    OP(V, lambda e: e.tensor_sub(PSIM[0:1, 0:512],
                                                 PSIM[0:1, 0:512],
                                                 rowtmp[0:1, 512:1024]),
                       reads=("pPSIM.0", "rowm2"), writes=("pPSIM.0",))
                    OP(SS, lambda e: e.activation(rowtmp[0:1, 512:1024],
                                                  PSIM[0:1, 0:512], AF.Ln),
                       reads=("pPSIM.0",), writes=("rowm2",))
                    OP(SS, lambda e: e.activation(rowtmp[0:1, 512:1024],
                                                  rowtmp[0:1, 512:1024], AF.Exp,
                                                  scale=-0.5),
                       reads=("rowm2",), writes=("rowm2",))
                    OP(V, lambda e: e.tensor_mul(rowtmp[0:1, 0:512],
                                                 rowtmp[0:1, 0:512],
                                                 rowtmp[0:1, 512:1024]),
                       reads=("rowmu", "rowm2"), writes=("rowmu",))
                    OP(T, lambda e: e.matmul(PMISC[:, :], onesrow[:],
                                             rowtmp[0:1, 512:1024],
                                             start=True, stop=True),
                       reads=("rowm2", "onesrow"), writes=("pPMISC",))
                    OP(V, lambda e, tt=tt:
                       e.tensor_copy(mu_s0[:, tt * 512:(tt + 1) * 512], PMISC[:, :]),
                       reads=("pPMISC",), writes=("mus0.%d" % tt,))
                    OP(T, lambda e: e.matmul(PMISC[:, :], onesrow[:],
                                             rowtmp[0:1, 0:512],
                                             start=True, stop=True),
                       reads=("rowmu", "onesrow"), writes=("pPMISC",))
                    OP(V, lambda e, tt=tt:
                       e.tensor_copy(mus0b[:, tt * 512:(tt + 1) * 512], PMISC[:, :]),
                       reads=("pPMISC",), writes=("mus0b.%d" % tt,))
        for c in range(CH):
            for hh in range(2):
                sl = (c * 2 + hh) % 2
                OP(SS if sl == 0 else Y, lambda e, c=c, hh=hh, sl=sl:
                   e.dma_start(out=xstage[:, sl * 2048:(sl + 1) * 2048],
                               in_=x_t[c * 128:(c + 1) * 128,
                                       hh * 2048:(hh + 1) * 2048]),
                   dma=True, dkey="xst%d" % sl,
                   writes=("xst%d" % sl,), tag="xload2")
                for ttl in range(4):
                    tt = hh * 4 + ttl
                    soff = c * SLAB + 1 + tt * 512
                    xoff = sl * 2048 + ttl * 512
                    OP(V, lambda e, so=soff, xo=xoff, tt=tt:
                       e.tensor_mul(S[:, so:so + 512], xstage[:, xo:xo + 512],
                                    mu_s0[:, tt * 512:(tt + 1) * 512]),
                       reads=("xst%d" % sl, "mus0.%d" % tt, "Szero"),
                       writes=("St.%d" % tt,))
                    OP(V, lambda e, so=soff, tt=tt:
                       e.tensor_sub(S[:, so:so + 512], S[:, so:so + 512],
                                    mus0b[:, tt * 512:(tt + 1) * 512]),
                       reads=("mus0b.%d" % tt,), writes=("St.%d" % tt, "sbuilt"))

    def emit_attn_branch():
        for c in range(CH):
            hi = c >= 4
            for k in range(CH):
                sl = k % 2
                soff = k * SLAB + 1 + c * 512
                OP(SS, lambda e, so=soff, sl=sl:
                   e.activation(b05scr[:, sl * 512:(sl + 1) * 512],
                                S[:, so:so + 512], AF.Copy),
                   reads=("St.%d" % c,), writes=("b05scr%d" % sl, "xst0"))
                if not hi:
                    OP(T, lambda e, k=k, sl=sl:
                       e.matmul(PQ[:, 0:512], awT_sb[:, k * 192:k * 192 + 128],
                                b05scr[:, sl * 512:(sl + 1) * 512],
                                start=(k == 0), stop=(k == 7)),
                       reads=("b05scr%d" % sl, "awT.%d" % k), writes=("pPQ.0",))
                    OP(T, lambda e, k=k, sl=sl:
                       e.matmul(PMISC[0:64, :], awT_sb[:, k * 192 + 128:k * 192 + 192],
                                b05scr[:, sl * 512:(sl + 1) * 512],
                                start=(k == 0), stop=(k == 7)),
                       reads=("b05scr%d" % sl, "awT.%d" % k), writes=("pPMISC",))
                else:
                    OP(T, lambda e, k=k, sl=sl:
                       e.matmul(PQ[64:128, 0:512], awT_sb[:, k * 192:k * 192 + 64],
                                b05scr[:, sl * 512:(sl + 1) * 512],
                                start=(k == 0), stop=(k == 7),
                                tile_position=(0, 64)),
                       reads=("b05scr%d" % sl, "awT.%d" % k), writes=("pPQ.0",))
                    OP(T, lambda e, k=k, sl=sl:
                       e.matmul(PQ[0:64, 512:1024], awT_sb[:, k * 192 + 64:k * 192 + 128],
                                b05scr[:, sl * 512:(sl + 1) * 512],
                                start=(k == 0), stop=(k == 7)),
                       reads=("b05scr%d" % sl, "awT.%d" % k), writes=("pPQ.1",))
                    OP(T, lambda e, k=k, sl=sl:
                       e.matmul(PMISC[64:128, :], awT_sb[:, k * 192 + 128:k * 192 + 192],
                                b05scr[:, sl * 512:(sl + 1) * 512],
                                start=(k == 0), stop=(k == 7),
                                tile_position=(0, 64)),
                       reads=("b05scr%d" % sl, "awT.%d" % k), writes=("pPMISC",))
            co = c * 512 if not hi else (c - 4) * 512
            if not hi:
                OP(V, lambda e, co=co: e.tensor_copy(qdm[0:64, co:co + 512],
                                                     PQ[0:64, 0:512]),
                   reads=("pPQ.0",), writes=("qdm.%d" % c,))
                OP(V, lambda e, co=co: e.tensor_copy(kdm[64:128, co:co + 512],
                                                     PQ[64:128, 0:512]),
                   reads=("pPQ.0",), writes=("kdm.%d" % c,))
                OP(V, lambda e, co=co: e.tensor_copy(vdm[0:64, co:co + 512],
                                                     PMISC[0:64, :]),
                   reads=("pPMISC",), writes=("vdm.%d" % c,))
            else:
                OP(V, lambda e, co=co: e.tensor_copy(qdm[64:128, co:co + 512],
                                                     PQ[64:128, 0:512]),
                   reads=("pPQ.0",), writes=("qdm.%d" % c,))
                OP(V, lambda e, co=co: e.tensor_copy(kdm[0:64, co:co + 512],
                                                     PQ[0:64, 512:1024]),
                   reads=("pPQ.1",), writes=("kdm.%d" % c,))
                OP(V, lambda e, co=co: e.tensor_copy(vdm[64:128, co:co + 512],
                                                     PMISC[64:128, :]),
                   reads=("pPMISC",), writes=("vdm.%d" % c,))
        for c in range(CH):
            hi = c >= 4
            kb_, vb_ = (64, 0) if not hi else (0, 64)
            co = c * 512 if not hi else (c - 4) * 512
            for blk in range(4):
                OP(T, lambda e, kb=kb_, co=co, blk=blk:
                   e.transpose(PTA[:, 0:64],
                               kdm[kb:kb + 64, co + blk * 128:co + (blk + 1) * 128],
                               ident_sb[kb:kb + 64, kb:kb + 64]),
                   reads=("kdm.%d" % c, "ident_sb"), writes=("pPTA",))
                OP(T, lambda e, vb=vb_, co=co, blk=blk:
                   e.transpose(PTA[:, 64:128],
                               vdm[vb:vb + 64, co + blk * 128:co + (blk + 1) * 128],
                               ident_sb[vb:vb + 64, vb:vb + 64]),
                   reads=("vdm.%d" % c, "ident_sb"), writes=("pPTA",))
                gb = c * 4 + blk
                OP(V, lambda e, gb=gb: e.tensor_copy(ktm2[:, gb * 64:(gb + 1) * 64],
                                                     PTA[:, 0:64]),
                   reads=("pPTA",), writes=("ktm2.%d" % gb,))
                OP(V, lambda e, gb=gb: e.tensor_copy(vtm2[:, gb * 64:(gb + 1) * 64],
                                                     PTA[:, 64:128]),
                   reads=("pPTA",), writes=("vtm2.%d" % gb,))
        for gb in range(32):
            OP(T, lambda e, gb=gb:
               e.matmul(PMISC[0:64, 0:64], ktm2[:, gb * 64:(gb + 1) * 64],
                        vtm2[:, gb * 64:(gb + 1) * 64],
                        start=(gb == 0), stop=(gb == 31)),
               reads=("ktm2.%d" % gb, "vtm2.%d" % gb), writes=("pPMISC",))
        OP(V, lambda e: e.tensor_copy(E_sb[0:64, :], PMISC[0:64, 0:64]),
           reads=("pPMISC",), writes=("E0",))
        OP(G, lambda e: e.dma_start(out=E_sb[64:128, :], in_=E_sb[0:64, :]),
           dma=True, dkey="edup", reads=("E0",), writes=("E1",), tag="Edup")
        for tt in range(8):
            hi = tt >= 4
            pb = 64 if hi else 0
            co = (tt - 4) * 512 if hi else tt * 512
            OP(T, lambda e, pb=pb, co=co, hi=hi:
               e.matmul(PQ[pb:pb + 64, 0:512], E_sb[pb:pb + 64, :],
                        qdm[pb:pb + 64, co:co + 512], start=True, stop=True,
                        tile_position=(64, 64) if hi else (0, 0)),
               reads=("E0", "E1", "qdm.%d" % (tt if not hi else tt)), writes=("pPQ.0",))
            OP(SS, lambda e, pb=pb, co=co:
               e.activation(ctx_sb[pb:pb + 64, co:co + 512], PQ[pb:pb + 64, 0:512],
                            AF.Copy, scale=CTXSCALE),
               reads=("pPQ.0",), writes=("ctx.%d" % tt,))

    def emit_layer_attn(l):
        pend_wo = [None]
        jglob = [0]

        def emit_wo(c):
            for od in range(CH):
                sl = od % 2
                OP(Y, lambda e, od=od, sl=sl:
                   e.dma_start(out=wor[:, sl * 512:(sl + 1) * 512],
                               in_=bass.AP(zWo.tensor,
                                           zWo.offset + ((l * 4) * 128) * DM + od * 128,
                                           [[DM, 128], [128 * DM, 4], [1, 128]])),
                   dma=True, dkey="wor%d" % sl,
                   reads=tuple("zWo.%d.%d.%d" % (l, k, od // 4) for k in range(4)),
                   writes=("wor%d" % sl,), tag="woL")
                def fn(e, od=od, sl=sl, c=c):
                    for k in range(4):
                        r = e.matmul(PQ[:, (od % 2) * 512:(od % 2) * 512 + 512],
                                     wor[:, sl * 512 + k * 128:sl * 512 + (k + 1) * 128],
                                     osb[:, k * 512:(k + 1) * 512],
                                     start=(k == 0), stop=(k == 3))
                    return r
                OP(T, fn, reads=("wor%d" % sl, "osb"),
                   writes=("pPQ.%d" % (od % 2),), multi=True)
                OP(V, lambda e, od=od, c=c:
                   e.tensor_add(S[:, od * SLAB + 1 + c * 512:od * SLAB + 1 + (c + 1) * 512],
                                S[:, od * SLAB + 1 + c * 512:od * SLAB + 1 + (c + 1) * 512],
                                PQ[:, (od % 2) * 512:(od % 2) * 512 + 512]),
                   reads=("pPQ.%d" % (od % 2),), writes=("St.%d" % c,))

        for c in range(CH):
            pump_prep(17, ce="G")
            OP(SS, lambda e, c=c:
               e.dma_start(out=tabst[:],
                           in_=bass.AP(uTab.tensor, uTab.offset + c * 512,
                                       [[NTOK, 128], [128 * NTOK, 4], [1, 512]])),
               dma=True, dkey="tabst", writes=("tabst",), tag="tab")
            for ti, (col_t, col_s) in enumerate(((0, qs_sb), (1, qsr_sb),
                                                 (2, ks_sb), (3, ksr_sb))):
                OP(SS, lambda e, ti=ti, cs=col_s:
                   e.activation(tabst[:, ti * 512:(ti + 1) * 512],
                                tabst[:, ti * 512:(ti + 1) * 512], AF.Copy,
                                scale=cs[:, l:l + 1]),
                   reads=("tabst", col_s.name), writes=("tabst", "tprep.%d" % ti,))
            if c > 0:
                OP(V, lambda e:
                   e.tensor_copy(bass.AP(krot.ap().tensor, krot.ap().offset,
                                         [[4096, 128], [1024, 4], [1, 512]]),
                                 bass.AP(krot.ap().tensor, krot.ap().offset + 512,
                                         [[4096, 128], [1024, 4], [1, 512]])),
                   reads=("krotB.0", "krotB.1", "krotB.2", "krotB.3"),
                   writes=("krotA",))
                OP(V, lambda e:
                   e.tensor_copy(bass.AP(vtmb.ap().tensor, vtmb.ap().offset,
                                         [[4096, 128], [512, 8], [1, 256]]),
                                 bass.AP(vtmb.ap().tensor, vtmb.ap().offset + 256,
                                         [[4096, 128], [512, 8], [1, 256]])),
                   reads=("vtmbB.0", "vtmbB.1", "vtmbB.2", "vtmbB.3"),
                   writes=("vtmbA",))
            for odg in range(2):
                def emit_qf(k, odg=odg):
                    sl = k % 2
                    s4 = k % 4
                    def qf(e, k=k, sl=sl, s4=s4):
                        for j in range(6):
                            r = e.matmul(PS_SLOT[j], wqkvr[:, sl * 768 + j * 128:
                                                           sl * 768 + (j + 1) * 128],
                                         scr[:, s4 * 512:(s4 + 1) * 512],
                                         start=(k == 0), stop=(k == 7))
                        return r
                    OP(T, qf, reads=("wqkvr%d" % sl, "scr%d" % s4),
                       writes=tuple(RS_SLOT[j] for j in range(6)), multi=True)
                for k in range(CH):
                    sl = k % 2
                    s4 = k % 4
                    if k >= 2:
                        emit_qf(k - 2)
                    OP(Y if k % 2 == 0 else G,
                       lambda e, k=k, sl=sl, odg=odg:
                       e.dma_start(out=wqkvr[:, sl * 768:(sl + 1) * 768],
                                   in_=zWqkv[l, k, :, odg * 768:(odg + 1) * 768]),
                       dma=True, dkey="wqkvr%d" % sl,
                       reads=tuple("zWqkv.%d.%d.%d" % (l, k, pi)
                                   for pi in ((0, 1) if odg == 0 else (1, 2))),
                       writes=("wqkvr%d" % sl,), tag="qkvw")
                    soff = k * SLAB + 1 + c * 512 - (1 if k >= 4 else 0)
                    rd = ["St.%d" % c] + (["St.%d" % (c - 1)] if (k >= 4 and c > 0) else [])
                    OP(SS, lambda e, so=soff, s4=s4:
                       e.activation(scr[:, s4 * 512:(s4 + 1) * 512],
                                    S[:, so:so + 512], AF.Copy),
                       reads=tuple(rd), writes=("scr%d" % s4,),
                       after=(MEMSET_S,))
                emit_qf(6)
                emit_qf(7)
                for j in range(6):
                    od = odg * 6 + j
                    if od < 4:
                        OP(V, lambda e, j=j, od=od:
                           e.tensor_copy(qrot[:, od * 512:(od + 1) * 512],
                                         PS_SLOT[j]),
                           reads=(RS_SLOT[j],), writes=("qrot.%d" % od,))
                    elif od < 8:
                        OP(V, lambda e, j=j, s=od - 4:
                           e.tensor_copy(krot[:, s * 1024 + 512:(s + 1) * 1024],
                                         PS_SLOT[j]),
                           reads=(RS_SLOT[j],), writes=("krotB.%d" % (od - 4),))
                    else:
                        OP(V, lambda e, j=j, od=od:
                           e.tensor_copy(vtmp[:, (od - 8) * 512:(od - 7) * 512],
                                         PS_SLOT[j]),
                           reads=(RS_SLOT[j],), writes=("vtmp.%d" % (od - 8),))
            grp = [[0, 1, 2], [3, 4, 5], [6, 7]]
            for gg in grp:
                sts = [qk_slab_stages(l, c, od) for od in gg]
                for row in zip(*sts):
                    for st in row:
                        st()
            for blk in range(4):
                for h in range(HEADS):
                    sv = h // 2
                    pb = (h % 2) * 64
                    OP(T, lambda e, sv=sv, pb=pb, blk=blk, h=h:
                       e.transpose(PTA[:, h * 64:h * 64 + 64],
                                   vtmp[pb:pb + 64, sv * 512 + blk * 128:
                                        sv * 512 + (blk + 1) * 128],
                                   ident_sb[pb:pb + 64, pb:pb + 64],
                                   tile_position=(pb, 0)),
                       reads=("vtmp.%d" % sv, "ident_sb"), writes=("pPTA",))
                OP(V, lambda e, blk=blk:
                   e.tensor_copy(bass.AP(vtmb.ap().tensor,
                                         vtmb.ap().offset + (4 + blk) * 64,
                                         [[4096, 128], [512, 8], [1, 64]]),
                                 PTA[:, 0:512]),
                   reads=("pPTA",), writes=("vtmbB.%d" % blk,))
            ptr_p = []
            pav_p = []
            for hp in range(4):
                for i in range(4):
                    for hh in range(2):
                        h = 2 * hp + hh
                        emit_sim(l, c, h, i, jglob)
                        ptr_p.append((l, c, h, i, jglob[0] - 1))
                        if len(ptr_p) > 1:
                            it = ptr_p.pop(0)
                            emit_tr(*it)
                            pav_p.append(it)
                        if len(pav_p) > 1:
                            emit_av2(*pav_p.pop(0))
            while ptr_p:
                it = ptr_p.pop(0)
                emit_tr(*it)
                pav_p.append(it)
            while pav_p:
                emit_av2(*pav_p.pop(0))
            if pend_wo[0] is not None:
                emit_wo(pend_wo[0])
            pend_wo[0] = c
        emit_wo(pend_wo[0])

    def qk_slab_stages(l, c, od):
        isq = od < 4
        s = od if isq else od - 4
        par = od % 3
        if isq:
            home = lambda: qrot[:, s * 512:(s + 1) * 512]
            qr = "qrot.%d" % s
        else:
            home = lambda: krot[:, s * 1024 + 512:(s + 1) * 1024]
            qr = "krotB.%d" % s
        tb = 0 if isq else 2
        sqsl = sqs[:, par * 512:(par + 1) * 512]
        rowt = (PMISC[0:2, :], PQ[0:2, 0:512], PSIM[0:2, 0:512])[par]
        rowr = ("pPMISC", "pPQ.0", "pPSIM.0")[par]
        bct = (PSIM[:, 1024:1536], PQ[:, 512:1024], PSIM[:, 512:1024])[par]
        bcr = ("pPSIM.2", "pPQ.1", "pPSIM.1")[par]
        lnt = bct[0:2, 0:512]
        rkb = (rowkb[0:2, :], rowkb2[0:2, :], rowkb3[0:2, :])[par]
        rkbr = "rowkb%d" % par
        m1sl = m1s[:, par * 512:(par + 1) * 512]
        m1r = "m1s%d" % par
        return [
            lambda: OP(SS, lambda e: e.activation(sqsl, home(), AF.Square),
                       reads=(qr,), writes=("sqs%d" % par,)),
            lambda: OP(T, lambda e: e.matmul(rowt, g2_sb[:], sqsl,
                                             start=True, stop=True),
                       reads=("sqs%d" % par, "g2_sb"), writes=(rowr,)),
            lambda: OP(V, lambda e: e.tensor_scalar_max(rowt, rowt, 1e-24),
                       reads=(rowr,), writes=(rowr,)),
            lambda: OP(SS, lambda e: e.activation(lnt, rowt, AF.Ln),
                       reads=(rowr,), writes=(bcr,)),
            lambda: OP(SS, lambda e: e.activation(rkb, lnt, AF.Exp, scale=-0.5),
                       reads=(bcr,), writes=(rkbr,)),
            lambda: OP(T, lambda e: e.matmul(bct, g2t_sb[:], rkb,
                                             start=True, stop=True),
                       reads=(rkbr, "g2t_sb"), writes=(bcr,)),
            lambda: OP(V, lambda e: e.tensor_mul(home(), home(), bct),
                       reads=(bcr,), writes=(qr,)),
            lambda: OP(T, lambda e: e.matmul(bct, perm_sb[:], home(),
                                             start=True, stop=True),
                       reads=(qr, "perm_sb"), writes=(bcr,)),
            lambda: OP(V, lambda e:
                       e.tensor_mul(m1sl, home(), tabst[:, tb * 512:(tb + 1) * 512]),
                       reads=(qr, "tabst", "tprep.%d" % tb), writes=(m1r,)),
            lambda: OP(V, lambda e:
                       e.tensor_mul(home(), bct,
                                    tabst[:, (tb + 1) * 512:(tb + 2) * 512]),
                       reads=(bcr, "tabst", "tprep.%d" % (tb + 1)), writes=(qr,)),
            lambda: OP(V, lambda e: e.tensor_add(home(), home(), m1sl),
                       reads=(m1r,), writes=(qr,)),
        ]

    def emit_sim(l, c, h, i, jglob):
        g = c * 4 + i
        nkb = min(g, 4) + 1
        width = nkb * 128
        j = jglob[0]; jglob[0] += 1
        par = j % 2
        s = h // 2
        pb = (h % 2) * 64
        slot_hi = 4 + i
        slot_lo = slot_hi - (nkb - 1)
        simt = PSIM if par == 0 else PQ
        simres = ("pPSIM.0", "pPSIM.1") if par == 0 else ("pPQ.0", "pPQ.1")
        stride = 1536 if par == 0 else 1024
        krd = ("qrot.%d" % s, "krotA", "krotB.%d" % s)

        def simfn(e, s=s, pb=pb, i=i, lo=slot_lo, w=width, simt=simt):
            off = 0
            r = None
            for seg in (512, 128):
                sw = min(seg, w - off)
                if sw <= 0:
                    break
                r = e.matmul(simt[:, off:off + sw],
                             qrot[pb:pb + 64, s * 512 + i * 128:s * 512 + (i + 1) * 128],
                             krot[pb:pb + 64, s * 1024 + lo * 128 + off:
                                  s * 1024 + lo * 128 + off + sw],
                             start=True, stop=True)
                off += sw
            return r
        OP(T, simfn, reads=krd, writes=simres, multi=True)
        if nkb == 5:
            OP(V, lambda e, simt=simt, stride=stride:
               e.tensor_add(bass.AP(simt.ap().tensor, simt.ap().offset,
                                    [[stride, 128], [512, 2], [1, 128]]),
                            bass.AP(simt.ap().tensor, simt.ap().offset,
                                    [[stride, 128], [512, 2], [1, 128]]),
                            bass.AP(mask_sb.ap().tensor, mask_sb.ap().offset,
                                    [[256, 128], [128, 2], [1, 128]])),
               reads=simres + ("mask_sb",), writes=simres)
        else:
            OP(V, lambda e, simt=simt, w=width:
               e.tensor_add(simt[:, w - 128:w], simt[:, w - 128:w],
                            mask_sb[:, 128:256]),
               reads=simres + ("mask_sb",), writes=simres)
        OP(V, lambda e, simt=simt, w=width, par=par:
           e.reduce_max(negm_sb[:, par:par + 1], simt[:, 0:w], axis=AX.X,
                        negate=True),
           reads=simres, writes=("negm%d" % par,))
        col = (h % 2) * 4 + i
        OP(SS, lambda e, simt=simt, w=width, par=par, col=col:
           e.activation(p_sb[:, par * 640:par * 640 + w], simt[:, 0:w], AF.Exp,
                        bias=negm_sb[:, par:par + 1],
                        accum_out=rs_sb[:, col:col + 1]),
           reads=simres + ("negm%d" % par,),
           writes=("p%d" % par, "rs.%d" % col))

    def emit_tr(l, c, h, i, j):
        g = c * 4 + i
        nkb = min(g, 4) + 1
        width = nkb * 128
        par = j % 2
        PT = PTA if par == 0 else PTB
        ptres = "pPTA" if par == 0 else "pPTB"

        def trfn(e, par=par, nkb=nkb, PT=PT):
            for t in range(nkb):
                r = e.transpose(PT[:, t * 128:(t + 1) * 128],
                                p_sb[:, par * 640 + t * 128:par * 640 + (t + 1) * 128],
                                ident_sb[:])
            return r
        OP(T, trfn, reads=("p%d" % par, "ident_sb"), writes=(ptres,), multi=True)
        if par == 0:
            OP(V, lambda e, w=width, par=par, PT=PT:
               e.tensor_copy(pt_sb[:, par * 640:par * 640 + w], PT[:, 0:w]),
               reads=(ptres,), writes=("pt%d" % par,))
        else:
            OP(SS, lambda e, w=width, par=par, PT=PT:
               e.activation(pt_sb[:, par * 640:par * 640 + w], PT[:, 0:w], AF.Copy),
               reads=(ptres,), writes=("pt%d" % par,))

    def emit_av2(l, c, h, i, j):
        g = c * 4 + i
        nkb = min(g, 4) + 1
        par = j % 2
        pb = (h % 2) * 64
        slot_lo = 4 + i - (nkb - 1)

        def avfn(e, h=h, i=i, pb=pb, lo=slot_lo, nkb=nkb, par=par):
            for t in range(nkb):
                r = e.matmul(PMISC[pb:pb + 64, i * 128:(i + 1) * 128],
                             vtmb[:, (h * 8 + lo + t) * 64:(h * 8 + lo + t + 1) * 64],
                             pt_sb[:, par * 640 + t * 128:par * 640 + (t + 1) * 128],
                             start=(t == 0), stop=(t == nkb - 1),
                             tile_position=(0, pb))
            return r
        own = tuple("vtmbB.%d" % b for b in range(max(0, slot_lo - 4), i + 1))
        OP(T, avfn, reads=("pt%d" % par, "vtmbA") + own, writes=("pPMISC",),
           multi=True)
        if h % 2 == 1 and i == 3:
            hp = h // 2
            OP(V, lambda e: e.reciprocal(rr_sb[:], rs_sb[:]),
               reads=tuple("rs.%d" % cc for cc in range(8)),
               writes=("rrf",))
            OP(V, lambda e: e.tensor_copy(rrb_sb[:], rr_sb[:]),
               reads=("rrf",), writes=("rrall",))
            rsrc = bass.AP(rrb_sb.ap().tensor, rrb_sb.ap().offset,
                           [[8, 128], [1, 8]])
            rdst = bass.AP(rrow.ap().tensor, rrow.ap().offset,
                           [[1024, 1], [8, 128], [1, 8]])
            OP(Y, lambda e, rsrc=rsrc, rdst=rdst:
               e.dma_start(out=rdst, in_=rsrc),
               dma=True, dkey="rrrow", reads=("rrall",), writes=("rrrow",),
               tag="rrd")
            for half in range(2):
                mv = bass.AP(rrow.ap().tensor,
                             rrow.ap().offset + half * 4,
                             [[1024, 1], [1, 4], [8, 128]])
                OP(T, lambda e, half=half, mv=mv:
                   e.matmul(PSIM[half * 64:half * 64 + 64, 1024:1536],
                            onesrow_b[0:1, 0:64], mv,
                            start=True, stop=True,
                            tile_position=(0, half * 64)),
                   reads=("rrrow", "onesrow_b"), writes=("pPSIM.2",))
            OP(V, lambda e: e.tensor_copy(m1s[:, 0:512], PSIM[:, 1024:1536]),
               reads=("pPSIM.2",), writes=("m1s0",))
            OP(V, lambda e, hp=hp:
               e.tensor_mul(osb[:, hp * 512:(hp + 1) * 512],
                            PMISC[:, :], m1s[:, 0:512]),
               reads=("pPMISC", "m1s0"), writes=("osb",))

    def emit_stats(l):
        INV = 1.0 / float(DM)
        for tt in range(8):
            for c in range(CH):
                sl = c % 2
                soff = c * SLAB + 1 + tt * 512 - (1 if c >= 4 else 0)
                rd = ["St.%d" % tt] + (["St.%d" % (tt - 1)] if (c >= 4 and tt > 0) else [])
                OP(SS, lambda e, so=soff, sl=sl:
                   e.activation(hcast[:, sl * 512:(sl + 1) * 512],
                                S[:, so:so + 512], AF.Copy),
                   reads=tuple(rd), writes=("hc%d" % sl,), after=(MEMSET_S,))
                OP(SS, lambda e, so=soff, sl=sl:
                   e.activation(sqcast[:, sl * 512:(sl + 1) * 512],
                                S[:, so:so + 512], AF.Square),
                   reads=tuple(rd), writes=("sq%d" % sl,), after=(MEMSET_S,))
                OP(T, lambda e, sl=sl, c=c:
                   e.matmul(PQ[:, 0:512], ident_sb[:],
                            hcast[:, sl * 512:(sl + 1) * 512],
                            start=(c == 0), stop=(c == 7)),
                   reads=("hc%d" % sl, "ident_sb"), writes=("pPQ.0",))
                OP(T, lambda e, sl=sl, c=c:
                   e.matmul(PQ[:, 512:1024], ident_sb[:],
                            sqcast[:, sl * 512:(sl + 1) * 512],
                            start=(c == 0), stop=(c == 7)),
                   reads=("sq%d" % sl, "ident_sb"), writes=("pPQ.1",))
            OP(V, lambda e: e.tensor_copy(foldt[:], PQ[:, 0:512]),
               reads=("pPQ.0",), writes=("foldt",))
            OP(V, lambda e: e.tensor_copy(foldt2[:], PQ[:, 512:1024]),
               reads=("pPQ.1",), writes=("foldt2",))
            OP(T, lambda e: e.matmul(PMISC[0:1, :], onescol[:], foldt[:],
                                     start=True, stop=True),
               reads=("foldt", "onescol"), writes=("pPMISC",))
            OP(T, lambda e: e.matmul(PSIM[0:1, 0:512], onescol[:], foldt2[:],
                                     start=True, stop=True),
               reads=("foldt2", "onescol"), writes=("pPSIM.0",))
            OP(V, lambda e: e.tensor_scalar_mul(rowtmp[0:1, 0:512],
                                                PMISC[0:1, :], INV),
               reads=("pPMISC",), writes=("rowmu",))
            OP(V, lambda e: e.tensor_mul(rowtmp[0:1, 512:1024],
                                         rowtmp[0:1, 0:512], rowtmp[0:1, 0:512]),
               reads=("rowmu",), writes=("rowm2",))
            OP(V, lambda e: e.tensor_scalar(PSIM[0:1, 0:512], PSIM[0:1, 0:512],
                                            INV, 1e-5, ALU.mult, ALU.add),
               reads=("pPSIM.0",), writes=("pPSIM.0",))
            OP(V, lambda e: e.tensor_sub(PSIM[0:1, 0:512], PSIM[0:1, 0:512],
                                         rowtmp[0:1, 512:1024]),
               reads=("pPSIM.0", "rowm2"), writes=("pPSIM.0",))
            OP(SS, lambda e: e.activation(rowtmp[0:1, 512:1024],
                                          PSIM[0:1, 0:512], AF.Ln),
               reads=("pPSIM.0",), writes=("rowm2",))
            OP(SS, lambda e: e.activation(rowtmp[0:1, 512:1024],
                                          rowtmp[0:1, 512:1024], AF.Exp,
                                          scale=-0.5),
               reads=("rowm2",), writes=("rowm2",))
            OP(G, lambda e, tt=tt: e.dma_start(out=zRows[l, tt], in_=rowtmp[0:1, :]),
               dma=True, dkey="zr%d" % l, reads=("rowmu", "rowm2"),
               writes=("zRows.%d.%d" % (l, tt),))

    def emit_ffn(l, after_quarter=None, per_idx=None):
        last_A = [None]
        last_B = [None]
        dcnt = [0]
        wring = [0]
        for qq in (3, 2, 1, 0):
            first_q = qq == 3
            for ttl in range(QTT):
                tt = qq * QTT + ttl
                OP(G, lambda e, l=l, tt=tt: e.dma_start(out=rowtmp[0:1, :],
                                                        in_=zRows[l, tt]),
                   dma=True, dkey="rowrest",
                   reads=("zRows.%d.%d" % (l, tt),), writes=("rowmu", "rowm2"))
                OP(T, lambda e: e.matmul(PQ[:, 0:512], onesrow[:],
                                         rowtmp[0:1, 0:512], start=True, stop=True),
                   reads=("rowmu", "onesrow"), writes=("pPQ.0",))
                OP(T, lambda e: e.matmul(PQ[:, 512:1024], onesrow[:],
                                         rowtmp[0:1, 512:1024], start=True, stop=True),
                   reads=("rowm2", "onesrow"), writes=("pPQ.1",))
                for k in range(CH):
                    soff = k * SLAB + 1 + tt * 512 - (1 if k >= 4 else 0)
                    rd = ["St.%d" % tt] + (["St.%d" % (tt - 1)] if (k >= 4 and tt > 0) else [])
                    a1 = OP(V, lambda e, so=soff:
                            e.tensor_sub(movtmp[:], S[:, so:so + 512], PQ[:, 0:512]),
                            reads=tuple(rd) + ("pPQ.0",), writes=("movtmp",),
                            after=(MEMSET_S, last_B[0]))
                    a2 = OP(V, lambda e, k=k, ttl=ttl:
                            e.tensor_mul(movc[:, k * 1024 + ttl * 512:
                                              k * 1024 + (ttl + 1) * 512],
                                         movtmp[:], PQ[:, 512:1024]),
                            reads=("movtmp", "pPQ.1"),
                            writes=("movc.%d.%d" % (k, ttl),))
                    last_A[0] = a2
            for rnd, ihs in enumerate(ROUNDS):
                pump_prep(7, ce="S")
                npc = len(ihs)
                def w1load(idx, ih, sl):
                    OP(Y, lambda e, ih=ih, sl=sl:
                       e.dma_start(out=w1r[:, sl * 2048:(sl + 1) * 2048],
                                   in_=zW1[l, ih]),
                       dma=True, dkey="w1r%d" % sl,
                       reads=tuple("zW1.%d.%d.%d" % (l, ih, pi) for pi in range(4)),
                       writes=("w1r%d" % sl,), tag="w1L")
                slmap = {}
                for idx in range(npc):
                    slmap[idx] = wring[0] % 3; wring[0] += 1
                for idx, ih in enumerate(ihs[:2]):
                    w1load(idx, ih, slmap[idx])
                for idx, ih in enumerate(ihs):
                    OP(G, lambda e, ih=ih, idx=idx:
                       e.dma_start(out=w2res[:, idx * 1024:(idx + 1) * 1024],
                                   in_=zW2[l, ih]),
                       dma=True, dkey="w2res%d" % idx,
                       reads=tuple("zW2.%d.%d.%d" % (l, ih, pi) for pi in range(2)),
                       writes=("w2res.%d" % idx,), tag="w2L")
                for idx, ih in enumerate(ihs):
                    if idx + 2 < npc:
                        w1load(idx + 2, ihs[idx + 2], slmap[idx + 2])
                    if per_idx is not None:
                        per_idx()
                    sl = slmap[idx]
                    if first_q:
                        for jj in range(2):
                            def c2fn(e, sl=sl, jj=jj, l=l):
                                for k in range(CH):
                                    r = e.matmul(PSIM[:, jj * 512:jj * 512 + 1],
                                                 w1r[:, sl * 2048 + k * 256 + jj * 128:
                                                     sl * 2048 + k * 256 + (jj + 1) * 128],
                                                 lnbb_sb[:, l * CH + k:l * CH + k + 1],
                                                 start=(k == 0), stop=(k == 7))
                                return r
                            OP(T, c2fn, reads=("w1r%d" % sl, "lnbb_sb"),
                               writes=("pPSIM.%d" % jj,), multi=True)
                            OP(V, lambda e, ih=ih, jj=jj, l=l:
                               e.tensor_add(c2_sb[:, ih * 2 + jj:ih * 2 + jj + 1],
                                            PSIM[:, jj * 512:jj * 512 + 1],
                                            b1_sb[:, (l * NIH + ih) * 2 + jj:
                                                  (l * NIH + ih) * 2 + jj + 1]),
                               reads=("pPSIM.%d" % jj, "b1_sb"),
                               writes=("c2.%d" % ih,))
                    for ttl in range(QTT):
                        gpar = (idx * QTT + ttl) % 2
                        aslot, gslot = (0, 1) if gpar == 0 else (2, 5)
                        for jj, slot in ((0, aslot), (1, gslot)):
                            def w1fn(e, sl=sl, jj=jj, ttl=ttl, slot=slot):
                                for k in range(CH):
                                    r = e.matmul(PS_SLOT[slot],
                                                 w1r[:, sl * 2048 + k * 256 + jj * 128:
                                                     sl * 2048 + k * 256 + (jj + 1) * 128],
                                                 movc[:, k * 1024 + ttl * 512:
                                                      k * 1024 + (ttl + 1) * 512],
                                                 start=(k == 0), stop=(k == 7))
                                return r
                            OP(T, w1fn,
                               reads=("w1r%d" % sl,) +
                                     tuple("movc.%d.%d" % (k, ttl) for k in range(CH)),
                               writes=(RS_SLOT[slot],), multi=True)
                        b = OP(SS, lambda e, gslot=gslot, gpar=gpar, ih=ih:
                               e.activation(gels[:, gpar * 512:(gpar + 1) * 512],
                                            PS_SLOT[gslot], AF.Gelu,
                                            bias=c2_sb[:, ih * 2 + 1:ih * 2 + 2]),
                               reads=(RS_SLOT[gslot], "c2.%d" % ih),
                               writes=("gels%d" % gpar,))
                        last_B[0] = b
                        OP(V, lambda e, aslot=aslot, gpar=gpar, ih=ih, idx=idx, ttl=ttl:
                           e.scalar_tensor_tensor(
                               gated[:, (idx * QTT + ttl) * 512:(idx * QTT + ttl + 1) * 512],
                               PS_SLOT[aslot], c2_sb[:, ih * 2:ih * 2 + 1],
                               gels[:, gpar * 512:(gpar + 1) * 512],
                               ALU.add, ALU.mult),
                           reads=(RS_SLOT[aslot], "gels%d" % gpar, "c2.%d" % ih),
                           writes=("gated.%d.%d" % (idx, ttl),))
                for ttl in range(QTT):
                    tt = qq * QTT + ttl
                    for od in range(CH):
                        wslot = od % 2
                        def w2fn(e, ttl=ttl, od=od, wslot=wslot, npc=npc):
                            for idx in range(npc):
                                r = e.matmul(PQ[:, wslot * 512:(wslot + 1) * 512],
                                             w2res[:, idx * 1024 + od * 128:
                                                   idx * 1024 + (od + 1) * 128],
                                             gated[:, (idx * QTT + ttl) * 512:
                                                   (idx * QTT + ttl + 1) * 512],
                                             start=(idx == 0), stop=(idx == npc - 1))
                            return r
                        OP(T, w2fn,
                           reads=tuple("w2res.%d" % idx for idx in range(npc)) +
                                 tuple("gated.%d.%d" % (idx, ttl) for idx in range(npc)),
                           writes=("pPQ.%d" % wslot,), multi=True)
                        ds = dcnt[0] % 3; dcnt[0] += 1
                        if rnd == 0:
                            OP(V, lambda e, wslot=wslot, ds=ds, od=od, l=l:
                               e.tensor_scalar_add(delta[:, ds * 512:(ds + 1) * 512],
                                                   PQ[:, wslot * 512:(wslot + 1) * 512],
                                                   b2_sb[:, l * CH + od:l * CH + od + 1]),
                               reads=("pPQ.%d" % wslot, "b2_sb"),
                               writes=("delta.%d" % ds,))
                        else:
                            OP(V, lambda e, wslot=wslot, ds=ds:
                               e.tensor_copy(delta[:, ds * 512:(ds + 1) * 512],
                                             PQ[:, wslot * 512:(wslot + 1) * 512]),
                               reads=("pPQ.%d" % wslot,), writes=("delta.%d" % ds,))
                        OP(G, lambda e, od=od, tt=tt, ds=ds:
                           e.tensor_add(S[:, od * SLAB + 1 + tt * 512:
                                          od * SLAB + 1 + (tt + 1) * 512],
                                        S[:, od * SLAB + 1 + tt * 512:
                                          od * SLAB + 1 + (tt + 1) * 512],
                                        delta[:, ds * 512:(ds + 1) * 512]),
                           reads=("delta.%d" % ds,), writes=("St.%d" % tt,))
            if after_quarter is not None:
                after_quarter(qq)

    final_thunks = []

    def pump_final(n):
        while n > 0 and final_thunks:
            final_thunks.pop(0)()
            n -= 1

    def emit_final_step(tt, od):
        if True:
                hi = tt >= 4
                pb = 64 if hi else 0
                co = (tt - 4) * 512 if hi else tt * 512
                sl = (od * 8 + tt) % 2
                OP(Y, lambda e, od=od, tt=tt, sl=sl:
                   e.dma_start(out=xst2[:, sl * 512:(sl + 1) * 512],
                               in_=x_t[od * 128:(od + 1) * 128,
                                       tt * 512:(tt + 1) * 512]),
                   dma=True, dkey="xst2.%d" % sl,
                   writes=("xst2.%d" % sl,), tag="xfin")
                OP(T, lambda e, od=od, pb=pb, co=co, sl=sl:
                   e.matmul(PQ[:, (sl % 2) * 512:(sl % 2) * 512 + 512],
                            woT_sb[pb:pb + 64, od * 128:(od + 1) * 128],
                            ctx_sb[pb:pb + 64, co:co + 512],
                            start=True, stop=True),
                   reads=("woT.0", "woT.1", "ctx.%d" % tt),
                   writes=("pPQ.%d" % (sl % 2),))
                OP(V, lambda e, od=od, tt=tt, sl=sl:
                   e.tensor_add(outsb[:, sl * 512:(sl + 1) * 512],
                                PQ[:, (sl % 2) * 512:(sl % 2) * 512 + 512],
                                S[:, od * SLAB + 1 + tt * 512:od * SLAB + 1 + (tt + 1) * 512]),
                   reads=("pPQ.%d" % (sl % 2), "St.%d" % tt),
                   writes=("outsb.%d" % sl,))
                OP(V, lambda e, sl=sl:
                   e.tensor_add(outsb[:, sl * 512:(sl + 1) * 512],
                                outsb[:, sl * 512:(sl + 1) * 512],
                                xst2[:, sl * 512:(sl + 1) * 512]),
                   reads=("xst2.%d" % sl,), writes=("outsb.%d" % sl,))
                OP(Y, lambda e, od=od, tt=tt, sl=sl:
                   e.dma_start(out=o_dm[od * 128:(od + 1) * 128,
                                        tt * 512:(tt + 1) * 512],
                               in_=outsb[:, sl * 512:(sl + 1) * 512]),
                   dma=True, dkey="odma", reads=("outsb.%d" % sl,),
                   writes=("odma",), tag="ostore")

    def emit_final_tt(tt):
        for od in range(CH):
            emit_final_step(tt, od)

    def queue_final_tt(tt):
        for od in range(CH):
            final_thunks.append(lambda tt=tt, od=od: emit_final_step(tt, od))

    def emit_final():
        for tt in range(8):
            emit_final_tt(tt)

    emit_prep_attnw()
    emit_prep_layer(0)
    emit_prep_layer(1)
    pump_until(42, ce="G")
    awlast = sch.last("S")
    m0 = len(sch.ops)
    emit_phase0()
    emit_attn_branch()
    sch.fence_from(m0, extra=(awlast,))
    for l in range(NL):
        ml = len(sch.ops)
        emit_layer_attn(l)
        pump_until(174 if l == 0 else 10**9, ce="G")
        sch.fence_from(ml)
        ms = len(sch.ops)
        emit_stats(l)
        if l == 0:
            emit_ffn(l)
            pump_until(206, ce="S")
            sch.fence_from(ms)
        else:
            def chase(qq):
                queue_final_tt(qq * 2)
                queue_final_tt(qq * 2 + 1)
                if qq == 0:
                    pump_final(10**9)
            emit_ffn(l, after_quarter=chase, per_idx=lambda: pump_final(2))
    mf = len(sch.ops)
    if dump is not None:
        src_t, width = dump[0], dump[1]
        tmap = {"S": S, "ctx": ctx_sb, "qrot": qrot, "krot": krot, "osb": osb,
                "movc": movc, "qdm": qdm, "kdm": kdm, "vdm": vdm, "E": E_sb,
                "mu_s0": mu_s0, "p": p_sb, "pt": pt_sb, "vtmb": vtmb,
                "gated": gated}
        tt_ = tmap[src_t]
        sch.fence()
        if tt_.dtype == F32:
            OP(Y, lambda e: e.dma_start(out=dbg, in_=tt_[:, 0:width]),
               dma=True, dkey="dbg", writes=("dbgo",), tag="dbg")
        else:
            nch = (width + 511) // 512
            for i in range(nch):
                w = min(512, width - i * 512)
                OP(V, lambda e, i=i, w=w:
                   e.tensor_copy(xst2[:, 0:w], tt_[:, i * 512:i * 512 + w]),
                   writes=("dbgt",))
                OP(Y, lambda e, i=i, w=w:
                   e.dma_start(out=dbg[:, i * 512:i * 512 + w], in_=xst2[:, 0:w]),
                   dma=True, dkey="dbg", reads=("dbgt",), writes=("dbgo",),
                   tag="dbg")
    OP(Y, lambda e: e.nop(), after=(sch.last("Y"),))

    sch.finalize()

    from contextlib import ExitStack
    allkeys = sorted({o["key"] for o in sch.ops if o["flag"]} |
                     set(Sched.TAPES))
    with ExitStack() as stack:
        sems = {k: stack.enter_context(nc.semaphore(name="sem_" + k.replace(".", "_")))
                for k in allkeys}
        block = stack.enter_context(nc.Block())

        @block.tensor
        def _(eng):
            sch.run_tape("T", eng, sems)

        @block.vector
        def _(eng):
            sch.run_tape("V", eng, sems)

        @block.scalar
        def _(eng):
            sch.run_tape("S", eng, sems)

        @block.gpsimd
        def _(eng):
            sch.run_tape("G", eng, sems)

        @block.sync
        def _(eng):
            sch.run_tape("Y", eng, sems)

    return nc


def kernel(**inputs):
    host = _prep_host(inputs)
    nc = build_program()
    x = np.asarray(inputs["L0_attn_x"])
    in_maps = []
    for b in range(NCORES):
        m = dict(host)
        m["x_t"] = np.ascontiguousarray(x[b].T)
        in_maps.append(m)
    res = run_bass_kernel_spmd(nc, in_maps, core_ids=list(range(NCORES)))
    out = np.empty((NCORES, NTOK, DM), dtype=np.float32)
    for b in range(NCORES):
        out[b] = res.results[b]["o_dm"].T
    return out


# version 25
# speedup vs baseline: 1.0885x; 1.0885x over previous
import math
import numpy as np
import ml_dtypes
import concourse.bass as bass
import concourse.mybir as mybir
from concourse.bass_utils import run_bass_kernel_spmd

F32 = mybir.dt.float32
BF16 = mybir.dt.bfloat16
AF = mybir.ActivationFunctionType
ALU = mybir.AluOpType
AX = mybir.AxisListType

BF = ml_dtypes.bfloat16

NCORES = 8
NTOK = 4096
DM = 1024
CH = 8
NT = 8
TT = 512
NL = 2
HEADS = 8
DH = 64
INNER = 512
HID = 2730
NIH = 22
HIDP = NIH * 128
WIN = 512
QT = NTOK // 128
SLAB = NTOK + 1
MASKNEG = -1e30
CTXSCALE = math.sqrt(3.0 / 64.0)

ROUNDS = [list(range(0, 4)), list(range(4, 8)), list(range(8, 12)),
          list(range(12, 16)), list(range(16, 19)), list(range(19, 22))]
NQ = 4
QTT = 2


class Sched:
    TAPES = ("T", "V", "S", "G", "Y")

    def __init__(self):
        self.ops = []
        self.tape_ops = {t: [] for t in self.TAPES}
        self.res = {}
        self.floor = []

    def op(self, tape, fn, reads=(), writes=(), dma=False, dkey=None,
           after=(), tag="", multi=False):
        if dma:
            assert dkey is not None, tag
            key = "D." + dkey
        else:
            key = tape
        oid = len(self.ops)
        deps = set()
        for a in after:
            if a is not None:
                deps.add(a)
        for r in reads:
            st = self.res.get(r)
            if st is not None and st["w"] is not None:
                deps.add(st["w"])
        for w in writes:
            st = self.res.get(w)
            if st is not None:
                if st["w"] is not None:
                    deps.add(st["w"])
                for r in st["r"]:
                    deps.add(r)
        deps.update(self.floor)
        deps.discard(oid)
        o = dict(id=oid, tape=tape, key=key, fn=fn, deps=deps, dma=dma,
                 flag=False, cnt=None, tag=tag, multi=multi)
        self.ops.append(o)
        self.tape_ops[tape].append(o)
        for r in reads:
            self.res.setdefault(r, {"w": None, "r": []})["r"].append(oid)
        for w in writes:
            st = self.res.setdefault(w, {"w": None, "r": []})
            st["w"] = oid
            st["r"] = []
        return oid

    def fence(self):
        self.floor = [t[-1]["id"] for t in self.tape_ops.values() if t]

    def fence_from(self, mark, extra=()):
        lasts = []
        for t in self.tape_ops.values():
            for o in reversed(t):
                if o["id"] >= mark:
                    if o["tag"].startswith("prep"):
                        continue
                    lasts.append(o["id"])
                    break
        lasts.extend(x for x in extra if x is not None)
        self.floor = lasts

    def last(self, tape):
        t = self.tape_ops[tape]
        return t[-1]["id"] if t else None

    def finalize(self):
        for o in self.ops:
            if o["dma"]:
                o["flag"] = True
            for d in o["deps"]:
                self.ops[d]["flag"] = True
        counts = {}
        for tape in self.TAPES:
            for o in self.tape_ops[tape]:
                if o["flag"]:
                    k = o["key"]
                    amt = 16 if o["dma"] else 1
                    counts[k] = counts.get(k, 0) + amt
                    o["cnt"] = counts[k]
        self.key_totals = counts
        prefix = {}
        atcnt = {}
        dma_running = {}
        for o in self.ops:
            need = {}
            for d in o["deps"]:
                do = self.ops[d]
                assert do["flag"] and do["cnt"] is not None, (do["tag"], do["tape"])
                k, c = do["key"], do["cnt"]
                if do["dma"]:
                    c = dma_running[k]
                if need.get(k, 0) < c:
                    need[k] = c
            cands = sorted(need.items())
            if len(cands) > 1:
                kept = list(cands)
                i = 0
                while i < len(kept):
                    k, c = kept[i]
                    cover = 0
                    for j, (k2, c2) in enumerate(kept):
                        if j == i:
                            continue
                        cover = max(cover, atcnt[(k2, c2)].get(k, 0))
                    if cover >= c:
                        kept.pop(i)
                    else:
                        i += 1
                cands = kept
            o["waits"] = cands
            know = {}
            for k, c in cands:
                for k2, c2 in atcnt[(k, c)].items():
                    if know.get(k2, 0) < c2:
                        know[k2] = c2
            if o["flag"]:
                k = o["key"]
                know[k] = max(know.get(k, 0), o["cnt"])
                pj = prefix.setdefault(k, {})
                for k2, c2 in know.items():
                    if pj.get(k2, 0) < c2:
                        pj[k2] = c2
                atcnt[(k, o["cnt"])] = dict(pj)
                if o["dma"]:
                    dma_running[k] = o["cnt"]

    def run_tape(self, tape, eng, sems):
        for o in self.tape_ops[tape]:
            if o["multi"] or len(o["waits"]) > 1:
                for k, v in o["waits"][:-1]:
                    eng.wait_ge(sems[k], v)
                if o["multi"]:
                    if o["waits"]:
                        k, v = o["waits"][-1]
                        eng.wait_ge(sems[k], v)
                    inst = o["fn"](eng)
                else:
                    inst = o["fn"](eng)
                    k, v = o["waits"][-1]
                    inst._wait_ge(sems[k], v)
            else:
                inst = o["fn"](eng)
                if o["waits"]:
                    k, v = o["waits"][0]
                    inst._wait_ge(sems[k], v)
            if o["flag"]:
                amt = 16 if o["dma"] else 1
                inst.then_inc(sems[o["key"]], amt)


def _pack_w1(w1):
    a = np.zeros((DM, 2 * HIDP), dtype=np.float32)
    a[:, :HID] = w1[:, :HID]
    a[:, HIDP:HIDP + HID] = w1[:, HID:]
    b = a.reshape(CH, 128, 2, NIH, 128)
    out = np.ascontiguousarray(b.transpose(3, 1, 0, 2, 4))
    return out.reshape(NIH, 128, 2048)


def _pack_w2(w2):
    a = np.zeros((HIDP, DM), dtype=np.float32)
    a[:HID] = w2
    b = a.reshape(NIH, 128, CH, 128)
    return np.ascontiguousarray(b).reshape(NIH, 128, DM)


def _pack_kout(w, dout):
    return np.ascontiguousarray(w.reshape(CH, 128, dout))


def _host_tables():
    half = np.arange(0, DH, 2, dtype=np.float64)
    inv_freq = 1.0 / (10000.0 ** (half / DH))
    t = np.arange(NTOK, dtype=np.float64)
    freqs = t[:, None] * inv_freq[None, :]
    freqs = np.concatenate([freqs, freqs], axis=1)
    base = (half + 0.4 * DH) / (1.4 * DH)
    power = (t - NTOK // 2) / float(WIN // 2)
    scale = base[None, :] ** power[:, None]
    scale = np.concatenate([scale, scale], axis=1)
    cos, sin = np.cos(freqs), np.sin(freqs)
    sign = np.ones((1, DH)); sign[0, :DH // 2] = -1.0
    t1q = (cos * scale).T
    t2q = (sign.T * (sin * scale).T)
    t1k = (cos / scale).T
    t2k = (sign.T * (sin / scale).T)
    def tile(x):
        return np.ascontiguousarray(
            np.concatenate([x, x], axis=0).astype(BF))
    return tile(t1q), tile(t2q), tile(t1k), tile(t2k)


def _host_masks():
    m = np.zeros((128, 256), dtype=BF)
    r = np.arange(128)
    lo = np.where(r[None, :] < r[:, None], MASKNEG, 0.0)
    hi = np.where(r[None, :] > r[:, None], MASKNEG, 0.0)
    m[:, :128] = lo.astype(BF)
    m[:, 128:] = hi.astype(BF)
    return m


def _prep_host(inputs):
    d = {}
    d["uWqkv"] = np.stack([_pack_kout(inputs["L0_ffn_wqkv"][l], 1536)
                           for l in range(NL)])
    d["uWo"] = np.stack([np.ascontiguousarray(
        inputs["L0_ffn_wo"][l].reshape(4, 128, 4, 2, 128)
        .transpose(2, 1, 3, 0, 4).reshape(4, 128, DM)) for l in range(NL)])
    d["uW1"] = np.stack([_pack_w1(inputs["L0_ffn_w1"][l]) for l in range(NL)])
    d["uW2"] = np.stack([_pack_w2(inputs["L0_ffn_w2"][l]) for l in range(NL)])
    aw = np.concatenate([inputs["L0_attn_wq"].T, inputs["L0_attn_wk"].T,
                         inputs["L0_attn_wv"].T], axis=1)
    d["uAw"] = np.ascontiguousarray(aw.reshape(CH, 128, 192))
    awo = inputs["L0_attn_wo"].T
    d["uAwo"] = np.ascontiguousarray(np.concatenate([awo, awo], axis=0))
    lng = inputs["L0_ffn_ln_g"]; lnb = inputs["L0_ffn_ln_b"]
    d["uLng"] = np.ascontiguousarray(
        np.stack([lng[l].reshape(CH, 128).T for l in range(NL)], 1).reshape(128, NL * CH))
    d["uLnb"] = np.ascontiguousarray(
        np.stack([lnb[l].reshape(CH, 128).T for l in range(NL)], 1).reshape(128, NL * CH))
    b1 = inputs["L0_ffn_b1"]
    b1p = np.zeros((NL, 2, HIDP), np.float32)
    b1p[:, 0, :HID] = b1[:, :HID]; b1p[:, 1, :HID] = b1[:, HID:]
    d["uB1"] = np.ascontiguousarray(
        b1p.reshape(NL, 2, NIH, 128).transpose(3, 0, 2, 1).reshape(128, NL * NIH * 2))
    b2 = inputs["L0_ffn_b2"]
    d["uB2"] = np.ascontiguousarray(
        b2.reshape(NL, CH, 128).transpose(2, 0, 1).reshape(128, NL * CH))
    qs = inputs["L0_ffn_q_scale"]; ks = inputs["L0_ffn_k_scale"]
    def qcols(x):
        c = np.tile(x, (1, 2))
        return np.ascontiguousarray(c.T)
    def qcols_rot(x):
        r = np.concatenate([x[:, DH // 2:], x[:, :DH // 2]], axis=1)
        return qcols(r)
    d["uQs"] = qcols(qs); d["uKs"] = qcols(ks)
    d["uQsr"] = qcols_rot(qs); d["uKsr"] = qcols_rot(ks)
    t1q, t2q, t1k, t2k = _host_tables()
    d["uTab"] = np.ascontiguousarray(np.stack([t1q, t2q, t1k, t2k]))
    d["uMask"] = _host_masks()
    d["uIdent"] = np.ascontiguousarray(np.eye(128, dtype=BF))
    g2t = np.zeros((2, 128), dtype=BF)
    g2t[0, :64] = 1; g2t[1, 64:] = 1
    d["uG2T"] = g2t
    g2 = np.zeros((128, 2), dtype=BF)
    g2[:64, 0] = 1; g2[64:, 1] = 1
    d["uG2"] = g2
    perm = np.zeros((128, 128), dtype=BF)
    for m in range(128):
        g = m // 64; dd = m % 64
        sd = dd + 32 if dd < 32 else dd - 32
        perm[g * 64 + sd, m] = 1
    d["uPerm"] = perm
    return d


def build_program(dump=None):
    nc = bass.Bass("TRN2", target_bir_lowering=False)

    def din(name, shape, dt=F32):
        return nc.dram_tensor(name, list(shape), dt, kind="ExternalInput").ap()

    x_t = din("x_t", (DM, NTOK))
    uWqkv = din("uWqkv", (NL, CH, 128, 1536))
    uWo = din("uWo", (NL, 4, 128, DM))
    uW1 = din("uW1", (NL, NIH, 128, 2048))
    uW2 = din("uW2", (NL, NIH, 128, DM))
    uAw = din("uAw", (CH, 128, 192))
    uAwo = din("uAwo", (128, DM))
    uLng = din("uLng", (128, NL * CH))
    uLnb = din("uLnb", (128, NL * CH))
    uB1 = din("uB1", (128, NL * NIH * 2))
    uB2 = din("uB2", (128, NL * CH))
    uQs = din("uQs", (128, NL)); uKs = din("uKs", (128, NL))
    uQsr = din("uQsr", (128, NL)); uKsr = din("uKsr", (128, NL))
    uTab = din("uTab", (4, 128, NTOK), BF16)
    uMask = din("uMask", (128, 256), BF16)
    uIdent = din("uIdent", (128, 128), BF16)
    uG2T = din("uG2T", (2, 128), BF16)
    uG2 = din("uG2", (128, 2), BF16)
    uPerm = din("uPerm", (128, 128), BF16)

    o_dm = nc.dram_tensor("o_dm", [DM, NTOK], F32, kind="ExternalOutput").ap()
    dbg = None
    if dump is not None:
        dbg = nc.dram_tensor("dbg", [128, dump[1]], F32, kind="ExternalOutput").ap()

    zWqkv = nc.dram_tensor("zWqkv", [NL, CH, 128, 1536], BF16, kind="Internal").ap()
    zWo = nc.dram_tensor("zWo", [NL, 4, 128, DM], BF16, kind="Internal").ap()
    zW1 = nc.dram_tensor("zW1", [NL, NIH, 128, 2048], BF16, kind="Internal").ap()
    zW2 = nc.dram_tensor("zW2", [NL, NIH, 128, DM], BF16, kind="Internal").ap()
    zRows = nc.dram_tensor("zRows", [NL, 8, 1024], F32, kind="Internal").ap()

    SA = nc.alloc_sbuf_tensor
    S = SA("S", [128, CH * SLAB], F32)
    ctx_sb = SA("ctx_sb", [128, 2048], BF16)
    woT_sb = SA("woT_sb", [128, DM], BF16)
    c2_sb = SA("c2_sb", [128, 2 * NIH], F32)
    mask_sb = SA("mask_sb", [128, 256], BF16)
    ident_sb = SA("ident_sb", [128, 128], BF16)
    perm_sb = SA("perm_sb", [128, 128], BF16)
    g2_sb = SA("g2_sb", [128, 2], BF16)
    g2t_sb = SA("g2t_sb", [2, 128], BF16)
    onesrow = SA("onesrow", [1, 128], F32)
    onescol = SA("onescol", [128, 1], BF16)
    onesrow_b = SA("onesrow_b", [1, 128], BF16)
    qs_sb = SA("qs_sb", [128, NL], F32); ks_sb = SA("ks_sb", [128, NL], F32)
    qsr_sb = SA("qsr_sb", [128, NL], F32); ksr_sb = SA("ksr_sb", [128, NL], F32)
    lng_sb = SA("lng_sb", [128, NL * CH], F32)
    lnb_sb = SA("lnb_sb", [128, NL * CH], F32)
    lnbb_sb = SA("lnbb_sb", [128, NL * CH], BF16)
    b1_sb = SA("b1_sb", [128, NL * NIH * 2], F32)
    b2_sb = SA("b2_sb", [128, NL * CH], F32)
    pstg = SA("pstg", [128, 2 * 512], F32)
    pout = SA("pout", [128, 2 * 512], BF16)
    rowtmp = SA("rowtmp", [1, 1024], F32)

    probe = SA("probe", [128, 8], F32)
    base = nc.lookup_mloc(probe).addr + 32
    REGION = 65536

    def mkcursor():
        cur = [0]
        def rg(name, shape, dt):
            per = int(np.prod(shape[1:])) * mybir.dt.size(dt)
            off = (cur[0] + 31) & ~31
            cur[0] = off + per
            assert cur[0] <= REGION, (name, off, per, cur[0])
            return nc.alloc_sbuf_tensor_at(name, list(shape), dt,
                                           offset=base + off)
        def at(name, shape, dt, off):
            per = int(np.prod(shape[1:])) * mybir.dt.size(dt)
            assert off + per <= REGION, (name, off, per)
            return nc.alloc_sbuf_tensor_at(name, list(shape), dt,
                                           offset=base + off)
        return rg, at, cur


    r0, at0, c0 = mkcursor()
    xstage = r0("xstage", [128, 2 * 2048], F32)
    qdm = r0("qdm", [128, 2048], BF16)
    kdm = r0("kdm", [128, 2048], BF16)
    vdm = r0("vdm", [128, 2048], BF16)
    ktm2 = r0("ktm2", [128, 2048], BF16)
    vtm2 = r0("vtm2", [128, 2048], BF16)
    E_sb = r0("E_sb", [128, 64], BF16)
    awT_sb = r0("awT_sb", [128, 1536], BF16)
    mu_s0 = r0("mu_s0", [128, 8 * TT], BF16)
    mus0b = r0("mus0b", [128, 8 * TT], BF16)
    b05scr = at0("b05scr", [128, 1024], BF16, 0)
    qdm_off = 16384
    p0cast = at0("p0cast", [128, 1024], BF16, qdm_off)
    p0sq = at0("p0sq", [128, 1024], BF16, qdm_off + 2048)
    foldt = at0("foldt", [128, 512], BF16, qdm_off + 4096)
    foldt2 = at0("foldt2", [128, 512], BF16, qdm_off + 5120)
    assert c0[0] <= REGION, c0

    ra, _, ca = mkcursor()
    wqkvr = ra("wqkvr", [128, 2 * 768], BF16)
    wor = ra("wor", [128, 2 * 1024], BF16)
    tabst = ra("tabst", [128, 4 * TT], BF16)
    qrot = ra("qrot", [128, 4 * TT], BF16)
    krot = ra("krot", [128, 4 * 1024], BF16)
    vtmb = ra("vtmb", [128, 4096], BF16)
    osb = ra("osb", [128, 4 * TT], BF16)
    scr = ra("scr", [128, 4 * 512], BF16)
    rrow = ra("rrow", [1, 1024], BF16)
    vtmp = ra("vtmp", [128, 4 * TT], BF16)
    m1s = ra("m1s", [128, 3 * TT], BF16)
    p_sb = ra("p_sb", [128, 2 * 640], BF16)
    pt_sb = ra("pt_sb", [128, 2 * 640], BF16)
    sqs = ra("sqs", [128, 3 * TT], BF16)
    rowkb = ra("rowkb", [2, 512], BF16)
    rowkb2 = ra("rowkb2", [2, 512], BF16)
    rowkb3 = ra("rowkb3", [2, 512], BF16)
    negm_sb = ra("negm_sb", [128, 2], F32)
    rs_sb = ra("rs_sb", [128, 8], F32)
    rr_sb = ra("rr_sb", [128, 8], F32)
    rrb_sb = ra("rrb_sb", [128, 8], BF16)
    assert ca[0] <= REGION, ca

    rf, atf, cf = mkcursor()
    movc = rf("movc", [128, CH * 1024], BF16)
    NPC = max(len(r) for r in ROUNDS)
    gated = rf("gated", [128, NPC * 2 * TT], BF16)
    w2res = rf("w2res", [128, NPC * DM], BF16)
    w1r = rf("w1r", [128, 3 * 2048], BF16)
    gels = rf("gels", [128, 2 * TT], BF16)
    delta = rf("delta", [128, 3 * TT], BF16)
    hcast = atf("hcast", [128, 2 * TT], BF16, 0)
    sqcast = atf("sqcast", [128, 2 * TT], BF16, 2048)
    movtmp = atf("movtmp", [128, TT], F32, 58368)
    assert cf[0] <= REGION, cf

    rz, _, cz = mkcursor()
    cz[0] = (cf[0] + 31) & ~31
    xst2 = rz("xst2", [128, 2 * TT], F32)
    outsb = rz("outsb", [128, 2 * TT], F32)
    assert cz[0] <= REGION, cz

    assert nc.sbuf_bytes_remaining >= 0

    PSIM = nc.alloc_psum_tensor("PSIM", [128, 1536], F32)
    PQ = nc.alloc_psum_tensor("PQ", [128, 1024], F32)
    PMISC = nc.alloc_psum_tensor("PMISC", [128, 512], F32)
    PTA = nc.alloc_psum_tensor("PTA", [128, 640], BF16)
    PTB = nc.alloc_psum_tensor("PTB", [128, 640], BF16)

    sch = Sched()
    OP = sch.op
    sch = Sched()
    OP = sch.op
    T, V, SS, G, Y = "T", "V", "S", "G", "Y"

    def d_name(t):
        return t.ap().tensor.name if hasattr(t, "ap") else t.tensor.name

    PS_SLOT = {0: PSIM[:, 0:512], 1: PSIM[:, 512:1024], 2: PSIM[:, 1024:1536],
               3: PQ[:, 0:512], 4: PQ[:, 512:1024], 5: PMISC[:, :]}
    RS_SLOT = {0: "pPSIM.0", 1: "pPSIM.1", 2: "pPSIM.2",
               3: "pPQ.0", 4: "pPQ.1", 5: "pPMISC"}

    for dst, src in ((mask_sb, uMask), (ident_sb, uIdent), (perm_sb, uPerm),
                     (g2_sb, uG2), (g2t_sb, uG2T), (qs_sb, uQs), (ks_sb, uKs),
                     (qsr_sb, uQsr), (ksr_sb, uKsr), (lng_sb, uLng),
                     (lnb_sb, uLnb), (b1_sb, uB1), (b2_sb, uB2)):
        OP(G, lambda e, d=dst, s=src: e.dma_start(out=d[:], in_=s),
           dma=True, dkey="smalls", writes=(dst.name,))
    OP(V, lambda e: e.memset(onesrow[:], 1.0), writes=("onesrow",))
    OP(V, lambda e: e.memset(onescol[:], 1.0), writes=("onescol",))
    OP(V, lambda e: e.memset(onesrow_b[:], 1.0), writes=("onesrow_b",))
    zap = bass.AP(S.ap().tensor, 0, [[SLAB * CH, 128], [SLAB, CH], [1, 1]])
    MEMSET_S = OP(V, lambda e: e.memset(zap, 0.0), writes=("Szero",))
    OP(G, lambda e: e.tensor_copy(out=lnbb_sb[:], in_=lnb_sb[:]),
       reads=("lnb_sb",), writes=("lnbb_sb",))

    prep_n = [0]
    prep_tasks = []

    def _prep_piece_now(src_ap, dst_ap, width, scale_ap, wres, store_key,
                        ce="G"):
        i = prep_n[0]; prep_n[0] += 1
        sl = (i % 2) * 512
        OP(G, lambda e, s=src_ap, o=sl, w=width:
           e.dma_start(out=pstg[:, o:o + w], in_=s),
           dma=True, dkey="pstg%d" % (i % 2),
           writes=("pstg%d" % (i % 2),), tag="prepL")
        ctape = {"S": SS, "V": V, "G": G}[ce]
        if dst_ap is None:
            home, rname = wres
            if ce == "S":
                OP(SS, lambda e, s=sl, w=width, h=home:
                   e.activation(h, pstg[:, s:s + w], AF.Copy),
                   reads=("pstg%d" % (i % 2),), writes=(rname,), tag="prepC")
            else:
                OP(ctape, lambda e, s=sl, w=width, h=home:
                   e.tensor_copy(h, pstg[:, s:s + w]),
                   reads=("pstg%d" % (i % 2),), writes=(rname,), tag="prepC")
            return
        if scale_ap is None:
            if ce == "S":
                OP(SS, lambda e, s=sl, w=width:
                   e.activation(pout[:, s:s + w], pstg[:, s:s + w], AF.Copy),
                   reads=("pstg%d" % (i % 2),), writes=("pout%d" % (i % 2),),
                   tag="prepC")
            else:
                OP(ctape, lambda e, s=sl, w=width:
                   e.tensor_copy(pout[:, s:s + w], pstg[:, s:s + w]),
                   reads=("pstg%d" % (i % 2),), writes=("pout%d" % (i % 2),),
                   tag="prepC")
        else:
            if ce == "S":
                def cfn(e, s=sl, w=width, scs=scale_ap):
                    n = len(scs); step = width // n
                    for t, sc in enumerate(scs):
                        r = e.activation(pout[:, s + t * step:s + (t + 1) * step],
                                         pstg[:, s + t * step:s + (t + 1) * step],
                                         AF.Copy, scale=sc)
                    return r
                OP(SS, cfn, reads=("pstg%d" % (i % 2),),
                   writes=("pout%d" % (i % 2),), tag="prepC", multi=True)
            else:
                def cfn(e, s=sl, w=width, scs=scale_ap):
                    n = len(scs); step = width // n
                    for t, sc in enumerate(scs):
                        r = e.tensor_scalar_mul(pout[:, s + t * step:s + (t + 1) * step],
                                                pstg[:, s + t * step:s + (t + 1) * step],
                                                sc)
                    return r
                OP(ctape, cfn, reads=("pstg%d" % (i % 2),),
                   writes=("pout%d" % (i % 2),), tag="prepC", multi=True)
        OP(G, lambda e, s=sl, w=width, d=dst_ap:
           e.dma_start(out=d, in_=pout[:, s:s + w]),
           dma=True, dkey=store_key, reads=("pout%d" % (i % 2),),
           writes=wres, tag="prepS")

    def prep_piece(*a, **kw):
        prep_tasks.append((a, kw))

    def pump_prep(n=10**9, ce="G"):
        k = 0
        while prep_tasks and k < n:
            a, kw = prep_tasks.pop(0)
            _prep_piece_now(*a, ce=ce, **(kw or {}))
            k += 1

    def pump_until(total, ce="G"):
        while prep_tasks and prep_n[0] < total:
            a, kw = prep_tasks.pop(0)
            _prep_piece_now(*a, ce=ce, **(kw or {}))

    def emit_prep_layer(l):
        for k in range(CH):
            for pi in range(3):
                prep_piece(uWqkv[l, k, :, pi * 512:(pi + 1) * 512],
                           zWqkv[l, k, :, pi * 512:(pi + 1) * 512], 512, None,
                           ("zWqkv.%d.%d.%d" % (l, k, pi),), "zqkv%d" % l)
        for k in range(4):
            for pi in range(2):
                prep_piece(uWo[l, k, :, pi * 512:(pi + 1) * 512],
                           zWo[l, k, :, pi * 512:(pi + 1) * 512], 512, None,
                           ("zWo.%d.%d.%d" % (l, k, pi),), "zwo%d" % l)
        for ih in range(NIH):
            for pi in range(4):
                scs = [lng_sb[:, l * CH + 2 * pi:l * CH + 2 * pi + 1],
                       lng_sb[:, l * CH + 2 * pi + 1:l * CH + 2 * pi + 2]]
                prep_piece(uW1[l, ih, :, pi * 512:(pi + 1) * 512],
                           zW1[l, ih, :, pi * 512:(pi + 1) * 512], 512, scs,
                           ("zW1.%d.%d.%d" % (l, ih, pi),), "zw1%d" % l)
        for ih in range(NIH):
            for pi in range(2):
                prep_piece(uW2[l, ih, :, pi * 512:(pi + 1) * 512],
                           zW2[l, ih, :, pi * 512:(pi + 1) * 512], 512, None,
                           ("zW2.%d.%d.%d" % (l, ih, pi),), "zw2%d" % l)

    def emit_prep_attnw():
        for k in range(CH):
            prep_piece(uAw[k], None, 192, None,
                       (awT_sb[:, k * 192:(k + 1) * 192], "awT.%d" % k), None)
        for pi in range(2):
            prep_piece(uAwo[:, pi * 512:(pi + 1) * 512], None, 512, None,
                       (woT_sb[:, pi * 512:(pi + 1) * 512], "woT.%d" % pi), None)

    def emit_phase0():
        INV = 1.0 / float(DM)
        for hh in range(2):
            for pr in range(2):
                tts = (hh * 4 + 2 * pr, hh * 4 + 2 * pr + 1)
                for c in range(CH):
                    sl = c % 2
                    OP(SS if sl == 0 else Y, lambda e, c=c, hh=hh, sl=sl:
                       e.dma_start(out=xstage[:, sl * 2048:(sl + 1) * 2048],
                                   in_=x_t[c * 128:(c + 1) * 128,
                                           hh * 2048:(hh + 1) * 2048]),
                       dma=True, dkey="xst%d" % sl,
                       writes=("xst%d" % sl,), tag="xload")
                    for q, tt in enumerate(tts):
                        xoff = sl * 2048 + (tt % 4) * 512
                        OP(SS, lambda e, xo=xoff, q=q, sl=sl:
                           e.activation(p0cast[:, q * 512:(q + 1) * 512],
                                        xstage[:, xo:xo + 512], AF.Copy),
                           reads=("xst%d" % sl,), writes=("p0c%d" % q,))
                        OP(SS, lambda e, xo=xoff, q=q, sl=sl:
                           e.activation(p0sq[:, q * 512:(q + 1) * 512],
                                        xstage[:, xo:xo + 512], AF.Square),
                           reads=("xst%d" % sl,), writes=("p0s%d" % q,))
                        OP(T, lambda e, q=q, c=c:
                           e.matmul(PS_SLOT[2 * q], ident_sb[:],
                                    p0cast[:, q * 512:(q + 1) * 512],
                                    start=(c == 0), stop=(c == 7)),
                           reads=("p0c%d" % q, "ident_sb"),
                           writes=(RS_SLOT[2 * q],))
                        OP(T, lambda e, q=q, c=c:
                           e.matmul(PS_SLOT[2 * q + 1], ident_sb[:],
                                    p0sq[:, q * 512:(q + 1) * 512],
                                    start=(c == 0), stop=(c == 7)),
                           reads=("p0s%d" % q, "ident_sb"),
                           writes=(RS_SLOT[2 * q + 1],))
                for q, tt in enumerate(tts):
                    OP(V, lambda e, q=q: e.tensor_copy(foldt[:], PS_SLOT[2 * q]),
                       reads=(RS_SLOT[2 * q],), writes=("foldt",))
                    OP(V, lambda e, q=q: e.tensor_copy(foldt2[:], PS_SLOT[2 * q + 1]),
                       reads=(RS_SLOT[2 * q + 1],), writes=("foldt2",))
                    OP(T, lambda e: e.matmul(PMISC[0:1, :], onescol[:], foldt[:],
                                             start=True, stop=True),
                       reads=("foldt", "onescol"), writes=("pPMISC",))
                    OP(T, lambda e: e.matmul(PSIM[0:1, 0:512], onescol[:], foldt2[:],
                                             start=True, stop=True),
                       reads=("foldt2", "onescol"), writes=("pPSIM.0",))
                    OP(V, lambda e: e.tensor_scalar_mul(rowtmp[0:1, 0:512],
                                                        PMISC[0:1, :], INV),
                       reads=("pPMISC",), writes=("rowmu",))
                    OP(V, lambda e: e.tensor_mul(rowtmp[0:1, 512:1024],
                                                 rowtmp[0:1, 0:512],
                                                 rowtmp[0:1, 0:512]),
                       reads=("rowmu",), writes=("rowm2",))
                    OP(V, lambda e: e.tensor_scalar(PSIM[0:1, 0:512],
                                                    PSIM[0:1, 0:512], INV, 1e-5,
                                                    ALU.mult, ALU.add),
                       reads=("pPSIM.0",), writes=("pPSIM.0",))
                    OP(V, lambda e: e.tensor_sub(PSIM[0:1, 0:512],
                                                 PSIM[0:1, 0:512],
                                                 rowtmp[0:1, 512:1024]),
                       reads=("pPSIM.0", "rowm2"), writes=("pPSIM.0",))
                    OP(SS, lambda e: e.activation(rowtmp[0:1, 512:1024],
                                                  PSIM[0:1, 0:512], AF.Ln),
                       reads=("pPSIM.0",), writes=("rowm2",))
                    OP(SS, lambda e: e.activation(rowtmp[0:1, 512:1024],
                                                  rowtmp[0:1, 512:1024], AF.Exp,
                                                  scale=-0.5),
                       reads=("rowm2",), writes=("rowm2",))
                    OP(V, lambda e: e.tensor_mul(rowtmp[0:1, 0:512],
                                                 rowtmp[0:1, 0:512],
                                                 rowtmp[0:1, 512:1024]),
                       reads=("rowmu", "rowm2"), writes=("rowmu",))
                    OP(T, lambda e: e.matmul(PMISC[:, :], onesrow[:],
                                             rowtmp[0:1, 512:1024],
                                             start=True, stop=True),
                       reads=("rowm2", "onesrow"), writes=("pPMISC",))
                    OP(V, lambda e, tt=tt:
                       e.tensor_copy(mu_s0[:, tt * 512:(tt + 1) * 512], PMISC[:, :]),
                       reads=("pPMISC",), writes=("mus0.%d" % tt,))
                    OP(T, lambda e: e.matmul(PMISC[:, :], onesrow[:],
                                             rowtmp[0:1, 0:512],
                                             start=True, stop=True),
                       reads=("rowmu", "onesrow"), writes=("pPMISC",))
                    OP(V, lambda e, tt=tt:
                       e.tensor_copy(mus0b[:, tt * 512:(tt + 1) * 512], PMISC[:, :]),
                       reads=("pPMISC",), writes=("mus0b.%d" % tt,))
        for c in range(CH):
            for hh in range(2):
                sl = (c * 2 + hh) % 2
                OP(SS if sl == 0 else Y, lambda e, c=c, hh=hh, sl=sl:
                   e.dma_start(out=xstage[:, sl * 2048:(sl + 1) * 2048],
                               in_=x_t[c * 128:(c + 1) * 128,
                                       hh * 2048:(hh + 1) * 2048]),
                   dma=True, dkey="xst%d" % sl,
                   writes=("xst%d" % sl,), tag="xload2")
                for ttl in range(4):
                    tt = hh * 4 + ttl
                    soff = c * SLAB + 1 + tt * 512
                    xoff = sl * 2048 + ttl * 512
                    OP(V, lambda e, so=soff, xo=xoff, tt=tt:
                       e.tensor_mul(S[:, so:so + 512], xstage[:, xo:xo + 512],
                                    mu_s0[:, tt * 512:(tt + 1) * 512]),
                       reads=("xst%d" % sl, "mus0.%d" % tt, "Szero"),
                       writes=("St.%d" % tt,))
                    OP(V, lambda e, so=soff, tt=tt:
                       e.tensor_sub(S[:, so:so + 512], S[:, so:so + 512],
                                    mus0b[:, tt * 512:(tt + 1) * 512]),
                       reads=("mus0b.%d" % tt,), writes=("St.%d" % tt, "sbuilt"))

    def emit_attn_branch():
        for c in range(CH):
            hi = c >= 4
            for k in range(CH):
                sl = k % 2
                soff = k * SLAB + 1 + c * 512
                OP(SS, lambda e, so=soff, sl=sl:
                   e.activation(b05scr[:, sl * 512:(sl + 1) * 512],
                                S[:, so:so + 512], AF.Copy),
                   reads=("St.%d" % c,), writes=("b05scr%d" % sl, "xst0"))
                if not hi:
                    OP(T, lambda e, k=k, sl=sl:
                       e.matmul(PQ[:, 0:512], awT_sb[:, k * 192:k * 192 + 128],
                                b05scr[:, sl * 512:(sl + 1) * 512],
                                start=(k == 0), stop=(k == 7)),
                       reads=("b05scr%d" % sl, "awT.%d" % k), writes=("pPQ.0",))
                    OP(T, lambda e, k=k, sl=sl:
                       e.matmul(PMISC[0:64, :], awT_sb[:, k * 192 + 128:k * 192 + 192],
                                b05scr[:, sl * 512:(sl + 1) * 512],
                                start=(k == 0), stop=(k == 7)),
                       reads=("b05scr%d" % sl, "awT.%d" % k), writes=("pPMISC",))
                else:
                    OP(T, lambda e, k=k, sl=sl:
                       e.matmul(PQ[64:128, 0:512], awT_sb[:, k * 192:k * 192 + 64],
                                b05scr[:, sl * 512:(sl + 1) * 512],
                                start=(k == 0), stop=(k == 7),
                                tile_position=(0, 64)),
                       reads=("b05scr%d" % sl, "awT.%d" % k), writes=("pPQ.0",))
                    OP(T, lambda e, k=k, sl=sl:
                       e.matmul(PQ[0:64, 512:1024], awT_sb[:, k * 192 + 64:k * 192 + 128],
                                b05scr[:, sl * 512:(sl + 1) * 512],
                                start=(k == 0), stop=(k == 7)),
                       reads=("b05scr%d" % sl, "awT.%d" % k), writes=("pPQ.1",))
                    OP(T, lambda e, k=k, sl=sl:
                       e.matmul(PMISC[64:128, :], awT_sb[:, k * 192 + 128:k * 192 + 192],
                                b05scr[:, sl * 512:(sl + 1) * 512],
                                start=(k == 0), stop=(k == 7),
                                tile_position=(0, 64)),
                       reads=("b05scr%d" % sl, "awT.%d" % k), writes=("pPMISC",))
            co = c * 512 if not hi else (c - 4) * 512
            if not hi:
                OP(V, lambda e, co=co: e.tensor_copy(qdm[0:64, co:co + 512],
                                                     PQ[0:64, 0:512]),
                   reads=("pPQ.0",), writes=("qdm.%d" % c,))
                OP(V, lambda e, co=co: e.tensor_copy(kdm[64:128, co:co + 512],
                                                     PQ[64:128, 0:512]),
                   reads=("pPQ.0",), writes=("kdm.%d" % c,))
                OP(V, lambda e, co=co: e.tensor_copy(vdm[0:64, co:co + 512],
                                                     PMISC[0:64, :]),
                   reads=("pPMISC",), writes=("vdm.%d" % c,))
            else:
                OP(V, lambda e, co=co: e.tensor_copy(qdm[64:128, co:co + 512],
                                                     PQ[64:128, 0:512]),
                   reads=("pPQ.0",), writes=("qdm.%d" % c,))
                OP(V, lambda e, co=co: e.tensor_copy(kdm[0:64, co:co + 512],
                                                     PQ[0:64, 512:1024]),
                   reads=("pPQ.1",), writes=("kdm.%d" % c,))
                OP(V, lambda e, co=co: e.tensor_copy(vdm[64:128, co:co + 512],
                                                     PMISC[64:128, :]),
                   reads=("pPMISC",), writes=("vdm.%d" % c,))
        for c in range(CH):
            hi = c >= 4
            kb_, vb_ = (64, 0) if not hi else (0, 64)
            co = c * 512 if not hi else (c - 4) * 512
            for blk in range(4):
                OP(T, lambda e, kb=kb_, co=co, blk=blk:
                   e.transpose(PTA[:, 0:64],
                               kdm[kb:kb + 64, co + blk * 128:co + (blk + 1) * 128],
                               ident_sb[kb:kb + 64, kb:kb + 64]),
                   reads=("kdm.%d" % c, "ident_sb"), writes=("pPTA",))
                OP(T, lambda e, vb=vb_, co=co, blk=blk:
                   e.transpose(PTA[:, 64:128],
                               vdm[vb:vb + 64, co + blk * 128:co + (blk + 1) * 128],
                               ident_sb[vb:vb + 64, vb:vb + 64]),
                   reads=("vdm.%d" % c, "ident_sb"), writes=("pPTA",))
                gb = c * 4 + blk
                OP(V, lambda e, gb=gb: e.tensor_copy(ktm2[:, gb * 64:(gb + 1) * 64],
                                                     PTA[:, 0:64]),
                   reads=("pPTA",), writes=("ktm2.%d" % gb,))
                OP(V, lambda e, gb=gb: e.tensor_copy(vtm2[:, gb * 64:(gb + 1) * 64],
                                                     PTA[:, 64:128]),
                   reads=("pPTA",), writes=("vtm2.%d" % gb,))
        for gb in range(32):
            OP(T, lambda e, gb=gb:
               e.matmul(PMISC[0:64, 0:64], ktm2[:, gb * 64:(gb + 1) * 64],
                        vtm2[:, gb * 64:(gb + 1) * 64],
                        start=(gb == 0), stop=(gb == 31)),
               reads=("ktm2.%d" % gb, "vtm2.%d" % gb), writes=("pPMISC",))
        OP(V, lambda e: e.tensor_copy(E_sb[0:64, :], PMISC[0:64, 0:64]),
           reads=("pPMISC",), writes=("E0",))
        OP(G, lambda e: e.dma_start(out=E_sb[64:128, :], in_=E_sb[0:64, :]),
           dma=True, dkey="edup", reads=("E0",), writes=("E1",), tag="Edup")
        for tt in range(8):
            hi = tt >= 4
            pb = 64 if hi else 0
            co = (tt - 4) * 512 if hi else tt * 512
            OP(T, lambda e, pb=pb, co=co, hi=hi:
               e.matmul(PQ[pb:pb + 64, 0:512], E_sb[pb:pb + 64, :],
                        qdm[pb:pb + 64, co:co + 512], start=True, stop=True,
                        tile_position=(64, 64) if hi else (0, 0)),
               reads=("E0", "E1", "qdm.%d" % (tt if not hi else tt)), writes=("pPQ.0",))
            OP(SS, lambda e, pb=pb, co=co:
               e.activation(ctx_sb[pb:pb + 64, co:co + 512], PQ[pb:pb + 64, 0:512],
                            AF.Copy, scale=CTXSCALE),
               reads=("pPQ.0",), writes=("ctx.%d" % tt,))

    def emit_layer_attn(l):
        pend_wo = [None]
        jglob = [0]

        def emit_wo(c):
            for od in range(CH):
                sl = (od // 2) % 2
                if od % 2 == 0:
                    OP(Y, lambda e, od=od, sl=sl:
                       e.dma_start(out=wor[:, sl * 1024:(sl + 1) * 1024],
                                   in_=zWo[l, od // 2]),
                       dma=True, dkey="wor%d" % sl,
                       reads=("zWo.%d.%d.0" % (l, od // 2),
                              "zWo.%d.%d.1" % (l, od // 2)),
                       writes=("wor%d" % sl,), tag="woL")
                def fn(e, od=od, sl=sl, c=c):
                    for k in range(4):
                        r = e.matmul(PQ[:, (od % 2) * 512:(od % 2) * 512 + 512],
                                     wor[:, sl * 1024 + (od % 2) * 512 + k * 128:
                                          sl * 1024 + (od % 2) * 512 + (k + 1) * 128],
                                     osb[:, k * 512:(k + 1) * 512],
                                     start=(k == 0), stop=(k == 3))
                    return r
                OP(T, fn, reads=("wor%d" % sl, "osb"),
                   writes=("pPQ.%d" % (od % 2),), multi=True)
                OP(V, lambda e, od=od, c=c:
                   e.tensor_add(S[:, od * SLAB + 1 + c * 512:od * SLAB + 1 + (c + 1) * 512],
                                S[:, od * SLAB + 1 + c * 512:od * SLAB + 1 + (c + 1) * 512],
                                PQ[:, (od % 2) * 512:(od % 2) * 512 + 512]),
                   reads=("pPQ.%d" % (od % 2),), writes=("St.%d" % c,))

        for c in range(CH):
            pump_prep(17, ce="G")
            OP(SS, lambda e, c=c:
               e.dma_start(out=tabst[:],
                           in_=bass.AP(uTab.tensor, uTab.offset + c * 512,
                                       [[NTOK, 128], [128 * NTOK, 4], [1, 512]])),
               dma=True, dkey="tabst", writes=("tabst",), tag="tab")
            for ti, (col_t, col_s) in enumerate(((0, qs_sb), (1, qsr_sb),
                                                 (2, ks_sb), (3, ksr_sb))):
                OP(SS, lambda e, ti=ti, cs=col_s:
                   e.activation(tabst[:, ti * 512:(ti + 1) * 512],
                                tabst[:, ti * 512:(ti + 1) * 512], AF.Copy,
                                scale=cs[:, l:l + 1]),
                   reads=("tabst", col_s.name), writes=("tabst", "tprep.%d" % ti,))
            if c > 0:
                OP(V, lambda e:
                   e.tensor_copy(bass.AP(krot.ap().tensor, krot.ap().offset,
                                         [[4096, 128], [1024, 4], [1, 512]]),
                                 bass.AP(krot.ap().tensor, krot.ap().offset + 512,
                                         [[4096, 128], [1024, 4], [1, 512]])),
                   reads=("krotB.0", "krotB.1", "krotB.2", "krotB.3"),
                   writes=("krotA",))
                OP(V, lambda e:
                   e.tensor_copy(bass.AP(vtmb.ap().tensor, vtmb.ap().offset,
                                         [[4096, 128], [512, 8], [1, 256]]),
                                 bass.AP(vtmb.ap().tensor, vtmb.ap().offset + 256,
                                         [[4096, 128], [512, 8], [1, 256]])),
                   reads=("vtmbB.0", "vtmbB.1", "vtmbB.2", "vtmbB.3"),
                   writes=("vtmbA",))
            for odg in range(2):
                def emit_qf(k, odg=odg):
                    sl = k % 2
                    s4 = k % 4
                    def qf(e, k=k, sl=sl, s4=s4):
                        for j in range(6):
                            r = e.matmul(PS_SLOT[j], wqkvr[:, sl * 768 + j * 128:
                                                           sl * 768 + (j + 1) * 128],
                                         scr[:, s4 * 512:(s4 + 1) * 512],
                                         start=(k == 0), stop=(k == 7))
                        return r
                    OP(T, qf, reads=("wqkvr%d" % sl, "scr%d" % s4),
                       writes=tuple(RS_SLOT[j] for j in range(6)), multi=True)
                for k in range(CH):
                    sl = k % 2
                    s4 = k % 4
                    if k >= 2:
                        emit_qf(k - 2)
                    OP(Y, lambda e, k=k, sl=sl, odg=odg:
                       e.dma_start(out=wqkvr[:, sl * 768:(sl + 1) * 768],
                                   in_=zWqkv[l, k, :, odg * 768:(odg + 1) * 768]),
                       dma=True, dkey="wqkvr%d" % sl,
                       reads=tuple("zWqkv.%d.%d.%d" % (l, k, pi)
                                   for pi in ((0, 1) if odg == 0 else (1, 2))),
                       writes=("wqkvr%d" % sl,), tag="qkvw")
                    soff = k * SLAB + 1 + c * 512 - (1 if k >= 4 else 0)
                    rd = ["St.%d" % c] + (["St.%d" % (c - 1)] if (k >= 4 and c > 0) else [])
                    OP(SS, lambda e, so=soff, s4=s4:
                       e.activation(scr[:, s4 * 512:(s4 + 1) * 512],
                                    S[:, so:so + 512], AF.Copy),
                       reads=tuple(rd), writes=("scr%d" % s4,),
                       after=(MEMSET_S,))
                emit_qf(6)
                emit_qf(7)
                for j in range(6):
                    od = odg * 6 + j
                    if od < 4:
                        OP(V, lambda e, j=j, od=od:
                           e.tensor_copy(qrot[:, od * 512:(od + 1) * 512],
                                         PS_SLOT[j]),
                           reads=(RS_SLOT[j],), writes=("qrot.%d" % od,))
                    elif od < 8:
                        OP(V, lambda e, j=j, s=od - 4:
                           e.tensor_copy(krot[:, s * 1024 + 512:(s + 1) * 1024],
                                         PS_SLOT[j]),
                           reads=(RS_SLOT[j],), writes=("krotB.%d" % (od - 4),))
                    else:
                        OP(V, lambda e, j=j, od=od:
                           e.tensor_copy(vtmp[:, (od - 8) * 512:(od - 7) * 512],
                                         PS_SLOT[j]),
                           reads=(RS_SLOT[j],), writes=("vtmp.%d" % (od - 8),))
            grp = [[0, 1, 2], [3, 4, 5], [6, 7]]
            for gg in grp:
                sts = [qk_slab_stages(l, c, od) for od in gg]
                for row in zip(*sts):
                    for st in row:
                        st()
            for blk in range(4):
                for h in range(HEADS):
                    sv = h // 2
                    pb = (h % 2) * 64
                    OP(T, lambda e, sv=sv, pb=pb, blk=blk, h=h:
                       e.transpose(PTA[:, h * 64:h * 64 + 64],
                                   vtmp[pb:pb + 64, sv * 512 + blk * 128:
                                        sv * 512 + (blk + 1) * 128],
                                   ident_sb[pb:pb + 64, pb:pb + 64],
                                   tile_position=(pb, 0)),
                       reads=("vtmp.%d" % sv, "ident_sb"), writes=("pPTA",))
                OP(V, lambda e, blk=blk:
                   e.tensor_copy(bass.AP(vtmb.ap().tensor,
                                         vtmb.ap().offset + (4 + blk) * 64,
                                         [[4096, 128], [512, 8], [1, 64]]),
                                 PTA[:, 0:512]),
                   reads=("pPTA",), writes=("vtmbB.%d" % blk,))
            ptr_p = []
            pav_p = []
            for hp in range(4):
                for i in range(4):
                    for hh in range(2):
                        h = 2 * hp + hh
                        emit_sim(l, c, h, i, jglob)
                        ptr_p.append((l, c, h, i, jglob[0] - 1))
                        if len(ptr_p) > 1:
                            it = ptr_p.pop(0)
                            emit_tr(*it)
                            pav_p.append(it)
                        if len(pav_p) > 1:
                            emit_av2(*pav_p.pop(0))
            while ptr_p:
                it = ptr_p.pop(0)
                emit_tr(*it)
                pav_p.append(it)
            while pav_p:
                emit_av2(*pav_p.pop(0))
            if pend_wo[0] is not None:
                emit_wo(pend_wo[0])
            pend_wo[0] = c
        emit_wo(pend_wo[0])

    def qk_slab_stages(l, c, od):
        isq = od < 4
        s = od if isq else od - 4
        par = od % 3
        if isq:
            home = lambda: qrot[:, s * 512:(s + 1) * 512]
            qr = "qrot.%d" % s
        else:
            home = lambda: krot[:, s * 1024 + 512:(s + 1) * 1024]
            qr = "krotB.%d" % s
        tb = 0 if isq else 2
        sqsl = sqs[:, par * 512:(par + 1) * 512]
        rowt = (PMISC[0:2, :], PQ[0:2, 0:512], PSIM[0:2, 0:512])[par]
        rowr = ("pPMISC", "pPQ.0", "pPSIM.0")[par]
        bct = (PSIM[:, 1024:1536], PQ[:, 512:1024], PSIM[:, 512:1024])[par]
        bcr = ("pPSIM.2", "pPQ.1", "pPSIM.1")[par]
        lnt = bct[0:2, 0:512]
        rkb = (rowkb[0:2, :], rowkb2[0:2, :], rowkb3[0:2, :])[par]
        rkbr = "rowkb%d" % par
        m1sl = m1s[:, par * 512:(par + 1) * 512]
        m1r = "m1s%d" % par
        return [
            lambda: OP(SS, lambda e: e.activation(sqsl, home(), AF.Square),
                       reads=(qr,), writes=("sqs%d" % par,)),
            lambda: OP(T, lambda e: e.matmul(rowt, g2_sb[:], sqsl,
                                             start=True, stop=True),
                       reads=("sqs%d" % par, "g2_sb"), writes=(rowr,)),
            lambda: OP(V, lambda e: e.tensor_scalar_max(rowt, rowt, 1e-24),
                       reads=(rowr,), writes=(rowr,)),
            lambda: OP(SS, lambda e: e.activation(lnt, rowt, AF.Ln),
                       reads=(rowr,), writes=(bcr,)),
            lambda: OP(SS, lambda e: e.activation(rkb, lnt, AF.Exp, scale=-0.5),
                       reads=(bcr,), writes=(rkbr,)),
            lambda: OP(T, lambda e: e.matmul(bct, g2t_sb[:], rkb,
                                             start=True, stop=True),
                       reads=(rkbr, "g2t_sb"), writes=(bcr,)),
            lambda: OP(V, lambda e: e.tensor_mul(home(), home(), bct),
                       reads=(bcr,), writes=(qr,)),
            lambda: OP(T, lambda e: e.matmul(bct, perm_sb[:], home(),
                                             start=True, stop=True),
                       reads=(qr, "perm_sb"), writes=(bcr,)),
            lambda: OP(V, lambda e:
                       e.tensor_mul(m1sl, home(), tabst[:, tb * 512:(tb + 1) * 512]),
                       reads=(qr, "tabst", "tprep.%d" % tb), writes=(m1r,)),
            lambda: OP(V, lambda e:
                       e.tensor_mul(home(), bct,
                                    tabst[:, (tb + 1) * 512:(tb + 2) * 512]),
                       reads=(bcr, "tabst", "tprep.%d" % (tb + 1)), writes=(qr,)),
            lambda: OP(V, lambda e: e.tensor_add(home(), home(), m1sl),
                       reads=(m1r,), writes=(qr,)),
        ]

    def emit_sim(l, c, h, i, jglob):
        g = c * 4 + i
        nkb = min(g, 4) + 1
        width = nkb * 128
        j = jglob[0]; jglob[0] += 1
        par = j % 2
        s = h // 2
        pb = (h % 2) * 64
        slot_hi = 4 + i
        slot_lo = slot_hi - (nkb - 1)
        simt = PSIM if par == 0 else PQ
        simres = ("pPSIM.0", "pPSIM.1") if par == 0 else ("pPQ.0", "pPQ.1")
        stride = 1536 if par == 0 else 1024
        krd = ("qrot.%d" % s, "krotA", "krotB.%d" % s)

        def simfn(e, s=s, pb=pb, i=i, lo=slot_lo, w=width, simt=simt):
            off = 0
            r = None
            for seg in (512, 128):
                sw = min(seg, w - off)
                if sw <= 0:
                    break
                r = e.matmul(simt[:, off:off + sw],
                             qrot[pb:pb + 64, s * 512 + i * 128:s * 512 + (i + 1) * 128],
                             krot[pb:pb + 64, s * 1024 + lo * 128 + off:
                                  s * 1024 + lo * 128 + off + sw],
                             start=True, stop=True)
                off += sw
            return r
        OP(T, simfn, reads=krd, writes=simres, multi=True)
        if nkb == 5:
            OP(V, lambda e, simt=simt, stride=stride:
               e.tensor_add(bass.AP(simt.ap().tensor, simt.ap().offset,
                                    [[stride, 128], [512, 2], [1, 128]]),
                            bass.AP(simt.ap().tensor, simt.ap().offset,
                                    [[stride, 128], [512, 2], [1, 128]]),
                            bass.AP(mask_sb.ap().tensor, mask_sb.ap().offset,
                                    [[256, 128], [128, 2], [1, 128]])),
               reads=simres + ("mask_sb",), writes=simres)
        else:
            OP(V, lambda e, simt=simt, w=width:
               e.tensor_add(simt[:, w - 128:w], simt[:, w - 128:w],
                            mask_sb[:, 128:256]),
               reads=simres + ("mask_sb",), writes=simres)
        OP(V, lambda e, simt=simt, w=width, par=par:
           e.reduce_max(negm_sb[:, par:par + 1], simt[:, 0:w], axis=AX.X,
                        negate=True),
           reads=simres, writes=("negm%d" % par,))
        col = (h % 2) * 4 + i
        OP(SS, lambda e, simt=simt, w=width, par=par, col=col:
           e.activation(p_sb[:, par * 640:par * 640 + w], simt[:, 0:w], AF.Exp,
                        bias=negm_sb[:, par:par + 1],
                        accum_out=rs_sb[:, col:col + 1]),
           reads=simres + ("negm%d" % par,),
           writes=("p%d" % par, "rs.%d" % col))

    def emit_tr(l, c, h, i, j):
        g = c * 4 + i
        nkb = min(g, 4) + 1
        width = nkb * 128
        par = j % 2
        PT = PTA if par == 0 else PTB
        ptres = "pPTA" if par == 0 else "pPTB"

        def trfn(e, par=par, nkb=nkb, PT=PT):
            for t in range(nkb):
                r = e.transpose(PT[:, t * 128:(t + 1) * 128],
                                p_sb[:, par * 640 + t * 128:par * 640 + (t + 1) * 128],
                                ident_sb[:])
            return r
        OP(T, trfn, reads=("p%d" % par, "ident_sb"), writes=(ptres,), multi=True)
        if par == 0:
            OP(V, lambda e, w=width, par=par, PT=PT:
               e.tensor_copy(pt_sb[:, par * 640:par * 640 + w], PT[:, 0:w]),
               reads=(ptres,), writes=("pt%d" % par,))
        else:
            OP(SS, lambda e, w=width, par=par, PT=PT:
               e.activation(pt_sb[:, par * 640:par * 640 + w], PT[:, 0:w], AF.Copy),
               reads=(ptres,), writes=("pt%d" % par,))

    def emit_av2(l, c, h, i, j):
        g = c * 4 + i
        nkb = min(g, 4) + 1
        par = j % 2
        pb = (h % 2) * 64
        slot_lo = 4 + i - (nkb - 1)

        def avfn(e, h=h, i=i, pb=pb, lo=slot_lo, nkb=nkb, par=par):
            for t in range(nkb):
                r = e.matmul(PMISC[pb:pb + 64, i * 128:(i + 1) * 128],
                             vtmb[:, (h * 8 + lo + t) * 64:(h * 8 + lo + t + 1) * 64],
                             pt_sb[:, par * 640 + t * 128:par * 640 + (t + 1) * 128],
                             start=(t == 0), stop=(t == nkb - 1),
                             tile_position=(0, pb))
            return r
        own = tuple("vtmbB.%d" % b for b in range(max(0, slot_lo - 4), i + 1))
        OP(T, avfn, reads=("pt%d" % par, "vtmbA") + own, writes=("pPMISC",),
           multi=True)
        if h % 2 == 1 and i == 3:
            hp = h // 2
            OP(V, lambda e: e.reciprocal(rr_sb[:], rs_sb[:]),
               reads=tuple("rs.%d" % cc for cc in range(8)),
               writes=("rrf",))
            OP(V, lambda e: e.tensor_copy(rrb_sb[:], rr_sb[:]),
               reads=("rrf",), writes=("rrall",))
            rsrc = bass.AP(rrb_sb.ap().tensor, rrb_sb.ap().offset,
                           [[8, 128], [1, 8]])
            rdst = bass.AP(rrow.ap().tensor, rrow.ap().offset,
                           [[1024, 1], [8, 128], [1, 8]])
            OP(Y, lambda e, rsrc=rsrc, rdst=rdst:
               e.dma_start(out=rdst, in_=rsrc),
               dma=True, dkey="rrrow", reads=("rrall",), writes=("rrrow",),
               tag="rrd")
            for half in range(2):
                mv = bass.AP(rrow.ap().tensor,
                             rrow.ap().offset + half * 4,
                             [[1024, 1], [1, 4], [8, 128]])
                OP(T, lambda e, half=half, mv=mv:
                   e.matmul(PSIM[half * 64:half * 64 + 64, 1024:1536],
                            onesrow_b[0:1, 0:64], mv,
                            start=True, stop=True,
                            tile_position=(0, half * 64)),
                   reads=("rrrow", "onesrow_b"), writes=("pPSIM.2",))
            OP(V, lambda e: e.tensor_copy(m1s[:, 0:512], PSIM[:, 1024:1536]),
               reads=("pPSIM.2",), writes=("m1s0",))
            OP(V, lambda e, hp=hp:
               e.tensor_mul(osb[:, hp * 512:(hp + 1) * 512],
                            PMISC[:, :], m1s[:, 0:512]),
               reads=("pPMISC", "m1s0"), writes=("osb",))

    def emit_stats(l):
        INV = 1.0 / float(DM)
        for tt in range(8):
            for c in range(CH):
                sl = c % 2
                soff = c * SLAB + 1 + tt * 512 - (1 if c >= 4 else 0)
                rd = ["St.%d" % tt] + (["St.%d" % (tt - 1)] if (c >= 4 and tt > 0) else [])
                OP(SS, lambda e, so=soff, sl=sl:
                   e.activation(hcast[:, sl * 512:(sl + 1) * 512],
                                S[:, so:so + 512], AF.Copy),
                   reads=tuple(rd), writes=("hc%d" % sl,), after=(MEMSET_S,))
                OP(SS, lambda e, so=soff, sl=sl:
                   e.activation(sqcast[:, sl * 512:(sl + 1) * 512],
                                S[:, so:so + 512], AF.Square),
                   reads=tuple(rd), writes=("sq%d" % sl,), after=(MEMSET_S,))
                OP(T, lambda e, sl=sl, c=c:
                   e.matmul(PQ[:, 0:512], ident_sb[:],
                            hcast[:, sl * 512:(sl + 1) * 512],
                            start=(c == 0), stop=(c == 7)),
                   reads=("hc%d" % sl, "ident_sb"), writes=("pPQ.0",))
                OP(T, lambda e, sl=sl, c=c:
                   e.matmul(PQ[:, 512:1024], ident_sb[:],
                            sqcast[:, sl * 512:(sl + 1) * 512],
                            start=(c == 0), stop=(c == 7)),
                   reads=("sq%d" % sl, "ident_sb"), writes=("pPQ.1",))
            OP(V, lambda e: e.tensor_copy(foldt[:], PQ[:, 0:512]),
               reads=("pPQ.0",), writes=("foldt",))
            OP(V, lambda e: e.tensor_copy(foldt2[:], PQ[:, 512:1024]),
               reads=("pPQ.1",), writes=("foldt2",))
            OP(T, lambda e: e.matmul(PMISC[0:1, :], onescol[:], foldt[:],
                                     start=True, stop=True),
               reads=("foldt", "onescol"), writes=("pPMISC",))
            OP(T, lambda e: e.matmul(PSIM[0:1, 0:512], onescol[:], foldt2[:],
                                     start=True, stop=True),
               reads=("foldt2", "onescol"), writes=("pPSIM.0",))
            OP(V, lambda e: e.tensor_scalar_mul(rowtmp[0:1, 0:512],
                                                PMISC[0:1, :], INV),
               reads=("pPMISC",), writes=("rowmu",))
            OP(V, lambda e: e.tensor_mul(rowtmp[0:1, 512:1024],
                                         rowtmp[0:1, 0:512], rowtmp[0:1, 0:512]),
               reads=("rowmu",), writes=("rowm2",))
            OP(V, lambda e: e.tensor_scalar(PSIM[0:1, 0:512], PSIM[0:1, 0:512],
                                            INV, 1e-5, ALU.mult, ALU.add),
               reads=("pPSIM.0",), writes=("pPSIM.0",))
            OP(V, lambda e: e.tensor_sub(PSIM[0:1, 0:512], PSIM[0:1, 0:512],
                                         rowtmp[0:1, 512:1024]),
               reads=("pPSIM.0", "rowm2"), writes=("pPSIM.0",))
            OP(SS, lambda e: e.activation(rowtmp[0:1, 512:1024],
                                          PSIM[0:1, 0:512], AF.Ln),
               reads=("pPSIM.0",), writes=("rowm2",))
            OP(SS, lambda e: e.activation(rowtmp[0:1, 512:1024],
                                          rowtmp[0:1, 512:1024], AF.Exp,
                                          scale=-0.5),
               reads=("rowm2",), writes=("rowm2",))
            OP(G, lambda e, tt=tt: e.dma_start(out=zRows[l, tt], in_=rowtmp[0:1, :]),
               dma=True, dkey="zr%d" % l, reads=("rowmu", "rowm2"),
               writes=("zRows.%d.%d" % (l, tt),))

    def emit_ffn(l, after_quarter=None, per_idx=None):
        last_A = [None]
        last_B = [None]
        dcnt = [0]
        wring = [0]
        for qq in (3, 2, 1, 0):
            first_q = qq == 3
            for ttl in range(QTT):
                tt = qq * QTT + ttl
                OP(G, lambda e, l=l, tt=tt: e.dma_start(out=rowtmp[0:1, :],
                                                        in_=zRows[l, tt]),
                   dma=True, dkey="rowrest",
                   reads=("zRows.%d.%d" % (l, tt),), writes=("rowmu", "rowm2"))
                OP(T, lambda e: e.matmul(PQ[:, 0:512], onesrow[:],
                                         rowtmp[0:1, 0:512], start=True, stop=True),
                   reads=("rowmu", "onesrow"), writes=("pPQ.0",))
                OP(T, lambda e: e.matmul(PQ[:, 512:1024], onesrow[:],
                                         rowtmp[0:1, 512:1024], start=True, stop=True),
                   reads=("rowm2", "onesrow"), writes=("pPQ.1",))
                for k in range(CH):
                    soff = k * SLAB + 1 + tt * 512 - (1 if k >= 4 else 0)
                    rd = ["St.%d" % tt] + (["St.%d" % (tt - 1)] if (k >= 4 and tt > 0) else [])
                    a1 = OP(V, lambda e, so=soff:
                            e.tensor_sub(movtmp[:], S[:, so:so + 512], PQ[:, 0:512]),
                            reads=tuple(rd) + ("pPQ.0",), writes=("movtmp",),
                            after=(MEMSET_S, last_B[0]))
                    a2 = OP(V, lambda e, k=k, ttl=ttl:
                            e.tensor_mul(movc[:, k * 1024 + ttl * 512:
                                              k * 1024 + (ttl + 1) * 512],
                                         movtmp[:], PQ[:, 512:1024]),
                            reads=("movtmp", "pPQ.1"),
                            writes=("movc.%d.%d" % (k, ttl),))
                    last_A[0] = a2
            for rnd, ihs in enumerate(ROUNDS):
                pump_prep(7, ce="S")
                npc = len(ihs)
                def w1load(idx, ih, sl):
                    OP(Y, lambda e, ih=ih, sl=sl:
                       e.dma_start(out=w1r[:, sl * 2048:(sl + 1) * 2048],
                                   in_=zW1[l, ih]),
                       dma=True, dkey="w1r%d" % sl,
                       reads=tuple("zW1.%d.%d.%d" % (l, ih, pi) for pi in range(4)),
                       writes=("w1r%d" % sl,), tag="w1L")
                slmap = {}
                for idx in range(npc):
                    slmap[idx] = wring[0] % 3; wring[0] += 1
                for idx, ih in enumerate(ihs[:2]):
                    w1load(idx, ih, slmap[idx])
                for idx, ih in enumerate(ihs):
                    OP(G, lambda e, ih=ih, idx=idx:
                       e.dma_start(out=w2res[:, idx * 1024:(idx + 1) * 1024],
                                   in_=zW2[l, ih]),
                       dma=True, dkey="w2res%d" % idx,
                       reads=tuple("zW2.%d.%d.%d" % (l, ih, pi) for pi in range(2)),
                       writes=("w2res.%d" % idx,), tag="w2L")
                for idx, ih in enumerate(ihs):
                    if idx + 2 < npc:
                        w1load(idx + 2, ihs[idx + 2], slmap[idx + 2])
                    if per_idx is not None:
                        per_idx()
                    sl = slmap[idx]
                    if first_q:
                        for jj in range(2):
                            def c2fn(e, sl=sl, jj=jj, l=l):
                                for k in range(CH):
                                    r = e.matmul(PSIM[:, jj * 512:jj * 512 + 1],
                                                 w1r[:, sl * 2048 + k * 256 + jj * 128:
                                                     sl * 2048 + k * 256 + (jj + 1) * 128],
                                                 lnbb_sb[:, l * CH + k:l * CH + k + 1],
                                                 start=(k == 0), stop=(k == 7))
                                return r
                            OP(T, c2fn, reads=("w1r%d" % sl, "lnbb_sb"),
                               writes=("pPSIM.%d" % jj,), multi=True)
                            OP(V, lambda e, ih=ih, jj=jj, l=l:
                               e.tensor_add(c2_sb[:, ih * 2 + jj:ih * 2 + jj + 1],
                                            PSIM[:, jj * 512:jj * 512 + 1],
                                            b1_sb[:, (l * NIH + ih) * 2 + jj:
                                                  (l * NIH + ih) * 2 + jj + 1]),
                               reads=("pPSIM.%d" % jj, "b1_sb"),
                               writes=("c2.%d" % ih,))
                    for ttl in range(QTT):
                        gpar = (idx * QTT + ttl) % 2
                        aslot, gslot = (0, 1) if gpar == 0 else (2, 5)
                        for jj, slot in ((0, aslot), (1, gslot)):
                            def w1fn(e, sl=sl, jj=jj, ttl=ttl, slot=slot):
                                for k in range(CH):
                                    r = e.matmul(PS_SLOT[slot],
                                                 w1r[:, sl * 2048 + k * 256 + jj * 128:
                                                     sl * 2048 + k * 256 + (jj + 1) * 128],
                                                 movc[:, k * 1024 + ttl * 512:
                                                      k * 1024 + (ttl + 1) * 512],
                                                 start=(k == 0), stop=(k == 7))
                                return r
                            OP(T, w1fn,
                               reads=("w1r%d" % sl,) +
                                     tuple("movc.%d.%d" % (k, ttl) for k in range(CH)),
                               writes=(RS_SLOT[slot],), multi=True)
                        b = OP(SS, lambda e, gslot=gslot, gpar=gpar, ih=ih:
                               e.activation(gels[:, gpar * 512:(gpar + 1) * 512],
                                            PS_SLOT[gslot], AF.Gelu,
                                            bias=c2_sb[:, ih * 2 + 1:ih * 2 + 2]),
                               reads=(RS_SLOT[gslot], "c2.%d" % ih),
                               writes=("gels%d" % gpar,))
                        last_B[0] = b
                        OP(V, lambda e, aslot=aslot, gpar=gpar, ih=ih, idx=idx, ttl=ttl:
                           e.scalar_tensor_tensor(
                               gated[:, (idx * QTT + ttl) * 512:(idx * QTT + ttl + 1) * 512],
                               PS_SLOT[aslot], c2_sb[:, ih * 2:ih * 2 + 1],
                               gels[:, gpar * 512:(gpar + 1) * 512],
                               ALU.add, ALU.mult),
                           reads=(RS_SLOT[aslot], "gels%d" % gpar, "c2.%d" % ih),
                           writes=("gated.%d.%d" % (idx, ttl),))
                for ttl in range(QTT):
                    tt = qq * QTT + ttl
                    for od in range(CH):
                        wslot = od % 2
                        def w2fn(e, ttl=ttl, od=od, wslot=wslot, npc=npc):
                            for idx in range(npc):
                                r = e.matmul(PQ[:, wslot * 512:(wslot + 1) * 512],
                                             w2res[:, idx * 1024 + od * 128:
                                                   idx * 1024 + (od + 1) * 128],
                                             gated[:, (idx * QTT + ttl) * 512:
                                                   (idx * QTT + ttl + 1) * 512],
                                             start=(idx == 0), stop=(idx == npc - 1))
                            return r
                        OP(T, w2fn,
                           reads=tuple("w2res.%d" % idx for idx in range(npc)) +
                                 tuple("gated.%d.%d" % (idx, ttl) for idx in range(npc)),
                           writes=("pPQ.%d" % wslot,), multi=True)
                        ds = dcnt[0] % 3; dcnt[0] += 1
                        if rnd == 0:
                            OP(V, lambda e, wslot=wslot, ds=ds, od=od, l=l:
                               e.tensor_scalar_add(delta[:, ds * 512:(ds + 1) * 512],
                                                   PQ[:, wslot * 512:(wslot + 1) * 512],
                                                   b2_sb[:, l * CH + od:l * CH + od + 1]),
                               reads=("pPQ.%d" % wslot, "b2_sb"),
                               writes=("delta.%d" % ds,))
                        else:
                            OP(V, lambda e, wslot=wslot, ds=ds:
                               e.tensor_copy(delta[:, ds * 512:(ds + 1) * 512],
                                             PQ[:, wslot * 512:(wslot + 1) * 512]),
                               reads=("pPQ.%d" % wslot,), writes=("delta.%d" % ds,))
                        OP(G, lambda e, od=od, tt=tt, ds=ds:
                           e.tensor_add(S[:, od * SLAB + 1 + tt * 512:
                                          od * SLAB + 1 + (tt + 1) * 512],
                                        S[:, od * SLAB + 1 + tt * 512:
                                          od * SLAB + 1 + (tt + 1) * 512],
                                        delta[:, ds * 512:(ds + 1) * 512]),
                           reads=("delta.%d" % ds,), writes=("St.%d" % tt,))
            if after_quarter is not None:
                after_quarter(qq)

    final_thunks = []

    def pump_final(n):
        while n > 0 and final_thunks:
            final_thunks.pop(0)()
            n -= 1

    def emit_final_step(tt, od):
        if True:
                hi = tt >= 4
                pb = 64 if hi else 0
                co = (tt - 4) * 512 if hi else tt * 512
                sl = (od * 8 + tt) % 2
                OP(Y, lambda e, od=od, tt=tt, sl=sl:
                   e.dma_start(out=xst2[:, sl * 512:(sl + 1) * 512],
                               in_=x_t[od * 128:(od + 1) * 128,
                                       tt * 512:(tt + 1) * 512]),
                   dma=True, dkey="xst2.%d" % sl,
                   writes=("xst2.%d" % sl,), tag="xfin")
                OP(T, lambda e, od=od, pb=pb, co=co, sl=sl:
                   e.matmul(PQ[:, (sl % 2) * 512:(sl % 2) * 512 + 512],
                            woT_sb[pb:pb + 64, od * 128:(od + 1) * 128],
                            ctx_sb[pb:pb + 64, co:co + 512],
                            start=True, stop=True),
                   reads=("woT.0", "woT.1", "ctx.%d" % tt),
                   writes=("pPQ.%d" % (sl % 2),))
                OP(V, lambda e, od=od, tt=tt, sl=sl:
                   e.tensor_add(outsb[:, sl * 512:(sl + 1) * 512],
                                PQ[:, (sl % 2) * 512:(sl % 2) * 512 + 512],
                                S[:, od * SLAB + 1 + tt * 512:od * SLAB + 1 + (tt + 1) * 512]),
                   reads=("pPQ.%d" % (sl % 2), "St.%d" % tt),
                   writes=("outsb.%d" % sl,))
                OP(V, lambda e, sl=sl:
                   e.tensor_add(outsb[:, sl * 512:(sl + 1) * 512],
                                outsb[:, sl * 512:(sl + 1) * 512],
                                xst2[:, sl * 512:(sl + 1) * 512]),
                   reads=("xst2.%d" % sl,), writes=("outsb.%d" % sl,))
                OP(Y, lambda e, od=od, tt=tt, sl=sl:
                   e.dma_start(out=o_dm[od * 128:(od + 1) * 128,
                                        tt * 512:(tt + 1) * 512],
                               in_=outsb[:, sl * 512:(sl + 1) * 512]),
                   dma=True, dkey="odma", reads=("outsb.%d" % sl,),
                   writes=("odma",), tag="ostore")

    def emit_final_tt(tt):
        for od in range(CH):
            emit_final_step(tt, od)

    def queue_final_tt(tt):
        for od in range(CH):
            final_thunks.append(lambda tt=tt, od=od: emit_final_step(tt, od))

    def emit_final():
        for tt in range(8):
            emit_final_tt(tt)

    emit_prep_attnw()
    emit_prep_layer(0)
    emit_prep_layer(1)
    pump_until(42, ce="G")
    awlast = sch.last("S")
    m0 = len(sch.ops)
    emit_phase0()
    emit_attn_branch()
    sch.fence_from(m0, extra=(awlast,))
    for l in range(NL):
        ml = len(sch.ops)
        emit_layer_attn(l)
        pump_until(174 if l == 0 else 10**9, ce="G")
        sch.fence_from(ml)
        ms = len(sch.ops)
        emit_stats(l)
        if l == 0:
            emit_ffn(l)
            pump_until(206, ce="S")
            sch.fence_from(ms)
        else:
            def chase(qq):
                queue_final_tt(qq * 2)
                queue_final_tt(qq * 2 + 1)
                if qq == 0:
                    pump_final(10**9)
            emit_ffn(l, after_quarter=chase, per_idx=lambda: pump_final(2))
    mf = len(sch.ops)
    if dump is not None:
        src_t, width = dump[0], dump[1]
        tmap = {"S": S, "ctx": ctx_sb, "qrot": qrot, "krot": krot, "osb": osb,
                "movc": movc, "qdm": qdm, "kdm": kdm, "vdm": vdm, "E": E_sb,
                "mu_s0": mu_s0, "p": p_sb, "pt": pt_sb, "vtmb": vtmb,
                "gated": gated}
        tt_ = tmap[src_t]
        sch.fence()
        if tt_.dtype == F32:
            OP(Y, lambda e: e.dma_start(out=dbg, in_=tt_[:, 0:width]),
               dma=True, dkey="dbg", writes=("dbgo",), tag="dbg")
        else:
            nch = (width + 511) // 512
            for i in range(nch):
                w = min(512, width - i * 512)
                OP(V, lambda e, i=i, w=w:
                   e.tensor_copy(xst2[:, 0:w], tt_[:, i * 512:i * 512 + w]),
                   writes=("dbgt",))
                OP(Y, lambda e, i=i, w=w:
                   e.dma_start(out=dbg[:, i * 512:i * 512 + w], in_=xst2[:, 0:w]),
                   dma=True, dkey="dbg", reads=("dbgt",), writes=("dbgo",),
                   tag="dbg")
    OP(Y, lambda e: e.nop(), after=(sch.last("Y"),))

    sch.finalize()

    from contextlib import ExitStack
    allkeys = sorted({o["key"] for o in sch.ops if o["flag"]} |
                     set(Sched.TAPES))
    with ExitStack() as stack:
        sems = {k: stack.enter_context(nc.semaphore(name="sem_" + k.replace(".", "_")))
                for k in allkeys}
        block = stack.enter_context(nc.Block())

        @block.tensor
        def _(eng):
            sch.run_tape("T", eng, sems)

        @block.vector
        def _(eng):
            sch.run_tape("V", eng, sems)

        @block.scalar
        def _(eng):
            sch.run_tape("S", eng, sems)

        @block.gpsimd
        def _(eng):
            sch.run_tape("G", eng, sems)

        @block.sync
        def _(eng):
            sch.run_tape("Y", eng, sems)

    return nc


def kernel(**inputs):
    host = _prep_host(inputs)
    nc = build_program()
    x = np.asarray(inputs["L0_attn_x"])
    in_maps = []
    for b in range(NCORES):
        m = dict(host)
        m["x_t"] = np.ascontiguousarray(x[b].T)
        in_maps.append(m)
    res = run_bass_kernel_spmd(nc, in_maps, core_ids=list(range(NCORES)))
    out = np.empty((NCORES, NTOK, DM), dtype=np.float32)
    for b in range(NCORES):
        out[b] = res.results[b]["o_dm"].T
    return out
